# version 02
# baseline (speedup 1.0000x reference)
import numpy as np
import ml_dtypes
import concourse.bass as bass
import concourse.mybir as mybir
from concourse.bass_utils import run_bass_kernel_spmd

F32 = mybir.dt.float32
BF16 = mybir.dt.bfloat16
AF = mybir.ActivationFunctionType
ALU = mybir.AluOpType
AXX = mybir.AxisListType.X
BF16NP = ml_dtypes.bfloat16

B, SEQ, D, H, HD = 2, 8192, 1024, 16, 64
NCORE = 8
HPC, HCOL = 4, 256
TPC = (B * SEQ) // NCORE
C = 128
E, EI = 8, 512
NGROUP = 4
GROUPS = [[0, 1, 2, 3], [4, 5, 6, 7]]
NEG = -30000.0
LOG2 = float(np.log(2.0))
DBG = 0
RACE = 0
PH = 8


class Sched:
    ENGS = ("t", "v", "s", "g", "y")
    NDS, NDG = 8, 4

    def __init__(self):
        self.streams = {e: [] for e in self.ENGS}
        self.cnt = {}
        self.dma_idx = {"y": 0, "g": 0}

    def dma(self, eng, fn, wait=()):
        i = self.dma_idx[eng]
        self.dma_idx[eng] = i + 1
        sem = ("dS%d" % (i % self.NDS)) if eng == "y" else ("dG%d" % (i % self.NDG))
        w = list(wait)
        prev = self.cnt.get(sem, 0)
        if prev > 0:
            w.append((sem, prev))
        self.emit(eng, fn, wait=w, inc=sem, inc_by=16)
        return (sem, self.cnt[sem])

    def n(self, sem):
        return self.cnt.get(sem, 0)

    def emit(self, eng, fn, wait=(), inc=None, inc_by=1):
        w = [(s, v) for (s, v) in wait if v > 0]
        if inc is not None:
            self.cnt[inc] = self.cnt.get(inc, 0) + inc_by
        self.streams[eng].append((fn, w, inc, inc_by))
        return self.cnt.get(inc, 0)

    def replay(self, eng, h, sems):
        for fn, waits, inc, inc_by in self.streams[eng]:
            for s, v in waits:
                h.wait_ge(sems[s], v)
            ins = fn(h)
            if inc is not None:
                if ins is None:
                    ins = h.nop()
                ins.then_inc(sems[inc], inc_by)


class Reg:

    def __init__(self, t, r0, c0, nr, ncl):
        self.t, self.r0, self.c0, self.nr, self.ncl = t, r0, c0, nr, ncl

    def __getitem__(self, idx):
        rs, cs = idx
        a = self.r0 + (rs.start or 0)
        b = self.r0 + (rs.stop if rs.stop is not None else self.nr)
        c = self.c0 + (cs.start or 0)
        d = self.c0 + (cs.stop if cs.stop is not None else self.ncl)
        return self.t[a:b, c:d]


def build_program():
    NCH = SEQ // C
    NTT = TPC // 128
    NTB = NTT // 4
    nc = bass.Bass("TRN2", target_bir_lowering=False, detect_race_conditions=bool(RACE))

    x_in = nc.dram_tensor("x_in", [TPC, D], F32, kind="ExternalInput").ap()
    wq = nc.dram_tensor("wq", [D, HCOL], F32, kind="ExternalInput").ap()
    wk = nc.dram_tensor("wk", [D, HCOL], F32, kind="ExternalInput").ap()
    wv = nc.dram_tensor("wv", [D, HCOL], F32, kind="ExternalInput").ap()
    wbf = nc.dram_tensor("wbf", [D, 2 * HPC], F32, kind="ExternalInput").ap()
    wo_sl = nc.dram_tensor("wo_sl", [HCOL, D], F32, kind="ExternalInput").ap()
    nw_sl = nc.dram_tensor("nw_sl", [1, HCOL], F32, kind="ExternalInput").ap()
    gw = nc.dram_tensor("gw", [D, E], F32, kind="ExternalInput").ap()
    gbias = nc.dram_tensor("gbias", [1, E], F32, kind="ExternalInput").ap()
    gp = nc.dram_tensor("gp", [E, D, EI], F32, kind="ExternalInput").ap()
    up = nc.dram_tensor("up", [E, D, EI], F32, kind="ExternalInput").ap()
    dp = nc.dram_tensor("dp", [E, EI, D], F32, kind="ExternalInput").ap()
    out = nc.dram_tensor("out", [TPC, D], F32, kind="ExternalOutput").ap()

    ag_in = nc.dram_tensor("ag_in", [D, TPC], BF16).ap()
    ag_out = nc.dram_tensor("ag_out", [4 * D, TPC], BF16).ap()
    rs_in = nc.dram_tensor("rs_in", [SEQ, D], BF16).ap()
    rs_out = nc.dram_tensor("rs_out", [TPC, D], BF16).ap()
    o_dram = nc.dram_tensor("o_dram", [SEQ, HCOL], BF16).ap()
    if DBG & 32:
        dbgX = nc.dram_tensor("dbgX", [128, 8 * TPC], BF16, kind="ExternalOutput").ap()
        dbgY = nc.dram_tensor("dbgY", [128, 16384], BF16, kind="ExternalOutput").ap()
    if DBG & 64:
        dbgG = nc.dram_tensor("dbgG", [128, 8 * (TPC // 128) + 1024], F32,
                              kind="ExternalOutput").ap()
    if DBG:
        dbgA = nc.dram_tensor("dbgA", [128, 12288], BF16).ap()
        dbgB = nc.dram_tensor("dbgB", [128, 2048], BF16).ap()
        dbgC = nc.dram_tensor("dbgC", [128, (TPC // 128) * 1024], F32).ap()
        dbgD = nc.dram_tensor("dbgD", [E * 128, (TPC // 128) * 1024], F32).ap()

    ident_b = nc.inline_tensor(np.eye(128, dtype=np.float32).astype(BF16NP), name="ident_b").ap()
    ident_f = nc.inline_tensor(np.eye(128, dtype=np.float32), name="ident_f").ap()
    ii = np.arange(128)
    mask1 = nc.inline_tensor(np.where(ii[None, :] <= ii[:, None], NEG, 0.0).astype(np.float32),
                             name="mask1").ap()
    mask2 = nc.inline_tensor(np.where(ii[None, :] >= ii[:, None], NEG, 0.0).astype(np.float32),
                             name="mask2").ap()
    incl01 = nc.inline_tensor((ii[:, None] <= ii[None, :]).astype(np.float32), name="incl01").ap()
    ones_row = nc.inline_tensor(np.ones((1, 128), np.float32), name="ones_row").ap()
    ones_c64 = nc.inline_tensor(np.ones((64, 1), np.float32), name="ones_c64").ap()
    epsv = nc.inline_tensor(np.tile(np.array([[1e-5, 1e-12, 1e-6, 0.0, 1e-20, 0.0, 0.0, 0.0]],
                                              np.float32), (128, 1)), name="epsv").ap()

    sc = Sched()
    E_, N_ = sc.emit, sc.n
    K = sc.cnt

    def toks(lst):
        d = {}
        for (sm, v) in lst:
            d[sm] = max(d.get(sm, 0), v)
        return list(d.items())

    import contextlib
    with contextlib.ExitStack() as _stk:
        _e = _stk.enter_context
        acc = _e(nc.sbuf_tensor("acc", [128, NTT * 1024], F32))
        xnh_f = _e(nc.sbuf_tensor("xnh_f", [128, 8 * TPC], BF16))
        ar16 = _e(nc.sbuf_tensor("ar16", [128, 30720], BF16))
        ar32 = _e(nc.sbuf_tensor("ar32", [128, 4096], F32))
        c_idb = _e(nc.sbuf_tensor("c_idb", [128, 128], BF16))
        c_idf = _e(nc.sbuf_tensor("c_idf", [128, 128], F32))
        c_m1 = _e(nc.sbuf_tensor("c_m1", [128, 128], F32))
        c_m2 = _e(nc.sbuf_tensor("c_m2", [128, 128], F32))
        c_incl = _e(nc.sbuf_tensor("c_incl", [128, 128], F32))
        c_or = _e(nc.sbuf_tensor("c_or", [1, 128], F32))
        c_oc = _e(nc.sbuf_tensor("c_oc", [64, 1], F32))
        c_eps = _e(nc.sbuf_tensor("c_eps", [128, 8], F32))
        wq_s = _e(nc.sbuf_tensor("wq_s", [128, 8 * HCOL], BF16))
        wkv_s = _e(nc.sbuf_tensor("wkv_s", [128, 8 * 512], BF16))
        wbf_s = _e(nc.sbuf_tensor("wbf_s", [128, 8 * 8], BF16))
        wo_s = _e(nc.sbuf_tensor("wo_s", [128, 2 * D], BF16))
        pass
        nwc = _e(nc.sbuf_tensor("nwc", [128, 2], F32))
        gwt_s = _e(nc.sbuf_tensor("gwt_s", [128, 8 * E], BF16))
        gwt_lo = _e(nc.sbuf_tensor("gwt_lo", [128, 8 * E], BF16))
        gwt_f = _e(nc.sbuf_tensor("gwt_f", [128, 8 * E], F32))
        gb_row = _e(nc.sbuf_tensor("gb_row", [1, E], F32))
        xtile = _e(nc.sbuf_tensor("xtile", [128, 1024], F32))
        pass
        lnw = _e(nc.sbuf_tensor("lnw", [128, 16], F32))
        cw_all = _e(nc.sbuf_tensor("cw_all", [128, 8 * NTT], F32))
        lblgT = _e(nc.sbuf_tensor("lblgT", [128, 8 * NCH], F32))
        d1a = _e(nc.sbuf_tensor("d1a", [128, 4 * NCH], F32))
        sbar = _e(nc.sbuf_tensor("sbar", [128, 4 * NCH], F32))
        shk = _e(nc.sbuf_tensor("shk", [128, 4 * NCH], F32))
        gga = _e(nc.sbuf_tensor("gga", [128, 4 * NCH], F32))
        gg2 = _e(nc.sbuf_tensor("gg2", [128, 4 * NCH], F32))
        dkb = _e(nc.sbuf_tensor("dkb", [128, 4 * NCH], F32))
        pass
        pass
        pass
        pass
        pass
        pass
        pass
        zer8 = _e(nc.sbuf_tensor("zer8", [8, 128], F32))
        pass
        pass
        esk = _e(nc.sbuf_tensor("esk", [128, 512], F32))
        pass
        knc = _e(nc.sbuf_tensor("knc", [128, 16], F32))
        pass
        pass
        pass
        pass
        pass
        pass
        pass
        pass
        pass
        pass
        pass
        pass
        pass
        pass
        pass
        pass
        pass
        pass
        pass
        pass
        pass
        pass
        pass
        msc = _e(nc.sbuf_tensor("msc", [128, 16], F32))
        pass
        pass
        pass
        pass
        pass
        pA = _e(nc.psum_tensor("pA", [128, 512], F32))
        pB = _e(nc.psum_tensor("pB", [128, 512], F32))
        pC = _e(nc.psum_tensor("pC", [128, 512], F32))
        pD = _e(nc.psum_tensor("pD", [128, 512], F32))
        pE = _e(nc.psum_tensor("pE", [128, 512], F32))
        pF = _e(nc.psum_tensor("pF", [128, 512], F32))
        pTb = _e(nc.psum_tensor("pTb", [128, 1024], BF16))
        pTb2 = _e(nc.psum_tensor("pTb2", [128, 1024], BF16))
        s_dS = [_e(nc.semaphore("dS%d" % i)) for i in range(Sched.NDS)]
        s_dG = [_e(nc.semaphore("dG%d" % i)) for i in range(Sched.NDG)]
        s_dC = _e(nc.semaphore("dC"))
        s_cT = _e(nc.semaphore("cT"))
        s_cV = _e(nc.semaphore("cV"))
        s_cS = _e(nc.semaphore("cS"))
        block = _e(nc.Block())
        wof = Reg(ar32, 0, 0, 128, 2048)
        zb8 = Reg(ar32, 0, 840, 8, 512)
        zb8b = Reg(ar32, 0, 1352, 8, 512)
        kmf = Reg(ar32, 0, 2048, 128, 256)
        ksq = Reg(ar32, 0, 2304, 128, 256)
        osq = Reg(ar32, 0, 2560, 64, 512)
        mrow = Reg(ar32, 0, 3072, 1, 4 * NCH)
        lgcrow = Reg(ar32, 0, 3328, 1, 4 * NCH)
        mnrow = Reg(ar32, 0, 3584, 1, 4 * NCH)
        zzrow = Reg(ar32, 0, 3840, 1, 4 * NCH)
        kvt = Reg(ar16, 0, 8192, 128, 2048)
        kck = Reg(ar16, 0, 10240, 128, 1024)
        knm = Reg(ar16, 0, 11264, 128, 1024)
        qft = Reg(ar16, 0, 12288, 128, 1024)
        kft = Reg(ar16, 0, 13312, 128, 1024)
        offb = Reg(ar16, 0, 14336, 64, 512)
        onr = Reg(ar16, 0, 14848, 128, 512)
        onf = Reg(ar16, 0, 15360, 128, 512)
        wop = Reg(ar16, 0, 15872, 128, 2048)
        rsb = Reg(ar16, 0, 17920, 128, 2048)
        xn32 = Reg(ar32, 0, 0, 128, 1024)
        swk = Reg(ar32, 0, 0, 128, 4 * NCH)
        cbt = Reg(ar32, 0, 256, 128, 128)
        ssf = Reg(ar32, 0, 384, 128, 128)
        xng = Reg(ar16, 0, 22336, 128, 2048)
        xx = Reg(ar16, 0, 19968, 128, 128)
        xxt = Reg(ar16, 0, 20096, 128, 128)
        a2 = Reg(ar16, 0, 20224, 128, 128)
        a2t = Reg(ar16, 0, 20352, 128, 128)
        a4 = Reg(ar16, 0, 20480, 128, 128)
        a4t = Reg(ar16, 0, 20608, 128, 128)
        a8t = Reg(ar16, 0, 20736, 128, 128)
        zz1 = Reg(ar16, 0, 20864, 128, 128)
        zz2 = Reg(ar16, 0, 20992, 128, 128)
        mt = Reg(ar16, 0, 21120, 128, 128)
        wfb = Reg(ar16, 0, 21248, 128, 128)
        uu = Reg(ar16, 0, 21376, 128, 64)
        ea_t = Reg(ar16, 0, 21440, 128, 256)
        ssb = Reg(ar16, 0, 21696, 128, 128)
        ont = Reg(ar16, 0, 21824, 128, 512)
        sems = {"dC": s_dC, "cT": s_cT, "cV": s_cV, "cS": s_cS}
        for i in range(Sched.NDS):
            sems["dS%d" % i] = s_dS[i]
        for i in range(Sched.NDG):
            sems["dG%d" % i] = s_dG[i]

        def _attach():
            @block.sync
            def _(y):
                sc.replay("y", y, sems)

            @block.gpsimd
            def _(g):
                sc.replay("g", g, sems)

            @block.vector
            def _(v):
                sc.replay("v", v, sems)

            @block.scalar
            def _(s):
                sc.replay("s", s, sems)

            @block.tensor
            def _(t):
                sc.replay("t", t, sems)

        def _early():
            wv, wt, ws = sc.n("cV"), sc.n("cT"), sc.n("cS")
            tk = []
            for i in range(NTT):
                tk.append(sc.dma("g", lambda g, i_=i: g.dma_start(
                    out=out[i_ * 128:(i_ + 1) * 128, :],
                    in_=acc[:, i_ * 1024:(i_ + 1) * 1024]),
                    wait=[("cV", wv), ("cT", wt), ("cS", ws)]))
            if DBG & 64:
                tk.append(sc.dma("y", lambda y: y.dma_start(
                    out=dbgG[:, 0:8 * NTT], in_=cw_all[:, :]),
                    wait=[("cV", wv), ("cT", wt), ("cS", ws)]))
                tk.append(sc.dma("y", lambda y: y.dma_start(
                    out=dbgG[:, 8 * NTT:8 * NTT + 1024], in_=ar32[:, 0:1024]),
                    wait=[("cV", wv), ("cT", wt), ("cS", ws)]))
            if DBG & 32:
                tk.append(sc.dma("y", lambda y: y.dma_start(
                    out=dbgX[:, :], in_=xnh_f[:, 0:8 * TPC]),
                    wait=[("cV", wv), ("cT", wt), ("cS", ws)]))
                tk.append(sc.dma("y", lambda y: y.dma_start(
                    out=dbgY[:, :], in_=ar16[:, 0:16384]),
                    wait=[("cV", wv), ("cT", wt), ("cS", ws)]))
            E_("g", lambda g: None, wait=toks(tk))
            _attach()
            return nc

        E_("v", lambda v: v.memset(lnw[:, :], 0.0), inc="cV")
        for _ in range(6):
            E_("v", lambda v: v.tensor_scalar(lnw[:, 0:1], lnw[:, 0:1], 1.0, None, ALU.add),
               inc="cV")

        tk_const = []
        for (dst, src) in [(c_idb, ident_b), (c_idf, ident_f), (c_m1, mask1),
                           (c_m2, mask2), (c_incl, incl01), (c_or, ones_row),
                           (c_oc, ones_c64), (c_eps, epsv), (gb_row, gbias)]:
            tk_const.append(sc.dma("y", lambda y, d=dst, s=src: y.dma_start(
                out=d[:, :], in_=s[:, :])))
        tk_const = toks(tk_const)
        tk_attnw = toks([
            sc.dma("g", lambda g: g.dma_start(
                out=wq_s[:, :].rearrange("p (dc j) -> p dc j", dc=8),
                in_=wq.rearrange("(dc p) j -> p dc j", p=128))),
            sc.dma("g", lambda g: g.dma_start(
                out=wkv_s[:, :].rearrange("p (dc x j) -> p dc x j", dc=8, x=2)[:, :, 0, :],
                in_=wk.rearrange("(dc p) j -> p dc j", p=128))),
            sc.dma("g", lambda g: g.dma_start(
                out=wkv_s[:, :].rearrange("p (dc x j) -> p dc x j", dc=8, x=2)[:, :, 1, :],
                in_=wv.rearrange("(dc p) j -> p dc j", p=128))),
            sc.dma("g", lambda g: g.dma_start(
                out=wbf_s[:, :].rearrange("p (dc j) -> p dc j", dc=8),
                in_=wbf.rearrange("(dc p) j -> p dc j", p=128))),
            sc.dma("g", lambda g: g.dma_start(
                out=gwt_s[:, :].rearrange("p (dc e) -> p dc e", dc=8),
                in_=gw.rearrange("(dc p) e -> p dc e", p=128)))])
        tk_w0 = toks([
            sc.dma("y", lambda y: y.dma_start(
                out=gwt_f[:, :].rearrange("p (dc e) -> p dc e", dc=8),
                in_=gw.rearrange("(dc p) e -> p dc e", p=128))),
            sc.dma("y", lambda y: y.dma_start(
                out=wof[:, :].rearrange("p (c j) -> p c j", c=2),
                in_=wo_sl.rearrange("(c p) j -> p c j", p=128))),
            sc.dma("y", lambda y: y.dma_start(
                out=nwc[:, :], in_=nw_sl.rearrange("o (c p) -> p (o c)", p=128),
                allow_slow_non_contiguous=True))])
        E_("v", lambda v: v.tensor_tensor(gwt_lo[:, :], gwt_f[:, :], gwt_s[:, :], ALU.subtract),
           wait=tk_w0 + tk_attnw, inc="cV")
        for cch in range(2):
            E_("v", lambda v, cc=cch: v.tensor_scalar(
                wo_s[:, cc * 1024:(cc + 1) * 1024], wof[:, cc * 1024:(cc + 1) * 1024],
                nwc[:, cc:cc + 1], None, ALU.mult), inc="cV")
        cv_w0s = N_("cV")
        E_("v", lambda v: v.memset(zer8[:, :], 0.0), inc="cV")


        if PH <= 0:
            return _early()

        XH, XL, XNL = 24576, 26624, 0
        xfree = {}
        for i in range(NTT):
            tb = 0
            w = [("cV", xfree[i - 1])] if i >= 1 else []
            tkx = sc.dma("y", lambda y, i_=i, t=tb: y.dma_start(
                out=xtile[:, t:t + 1024], in_=x_in[i_ * 128:(i_ + 1) * 128, :]), wait=w)
            w = [tkx]
            E_("v", lambda v, t=tb: v.reduce_sum(lnw[:, 0:1], xtile[:, t:t + 1024], AXX),
               wait=w, inc="cV")
            cv_red1 = N_("cV")
            E_("s", lambda s, t=tb: s.activation(xn32[:, :], xtile[:, t:t + 1024], AF.Square),
               wait=w + ([("cV", K.get(("hilo", i - 1), 0))] if i >= 1 else [("cV", cv_w0s)]),
               inc="cS")
            cs_sq = N_("cS")
            E_("v", lambda v: v.reduce_sum(lnw[:, 1:2], xn32[:, :], AXX),
               wait=[("cS", cs_sq)], inc="cV")
            cv_red2 = N_("cV")
            E_("v", lambda v: v.tensor_scalar(lnw[:, 3:4], lnw[:, 1:2], 1.0 / D, None, ALU.mult),
               wait=[("cV", cv_red2)], inc="cV")
            E_("v", lambda v: v.tensor_scalar(lnw[:, 2:3], lnw[:, 0:1], 1.0 / D, None, ALU.mult),
               wait=[("cV", cv_red1)], inc="cV")
            cv_mean = N_("cV")
            E_("s", lambda s: s.activation(lnw[:, 4:5], lnw[:, 2:3], AF.Square),
               wait=[("cV", cv_mean)], inc="cS")
            cs_m2 = N_("cS")
            E_("v", lambda v: v.tensor_tensor(lnw[:, 5:6], lnw[:, 3:4], lnw[:, 4:5], ALU.subtract),
               wait=[("cS", cs_m2)], inc="cV")
            cv_var = N_("cV")
            E_("s", lambda s: s.activation(lnw[:, 6:7], lnw[:, 5:6], AF.Ln, bias=c_eps[:, 0:1], scale=1.0),
               wait=[("cV", cv_var)], inc="cS")
            E_("s", lambda s: s.activation(lnw[:, 7:8], lnw[:, 6:7], AF.Exp, bias=0.0, scale=-0.5),
               inc="cS")
            cs_rstd = N_("cS")
            E_("v", lambda v, t=tb: v.tensor_scalar(
                xn32[:, :], xtile[:, t:t + 1024], lnw[:, 2:3], lnw[:, 7:8],
                ALU.subtract, ALU.mult), wait=[("cS", cs_rstd)], inc="cV")
            E_("v", lambda v, t=tb, i_=i: v.tensor_copy(
                acc[:, i_ * 1024:(i_ + 1) * 1024], xtile[:, t:t + 1024]), inc="cV")
            xfree[i] = N_("cV")
            ho, lo = XH + tb, XL + tb
            E_("v", lambda v, h_=ho: v.tensor_copy(ar16[:, h_:h_ + 1024], xn32[:, :]),
               wait=[("cT", K.get(("lntp", i - 2), 0))], inc="cV")
            E_("v", lambda v, h_=ho, l_=lo: v.tensor_tensor(
                ar16[:, l_:l_ + 1024], xn32[:, :], ar16[:, h_:h_ + 1024], ALU.subtract), inc="cV")
            K[("hilo", i)] = N_("cV")
            for (reg, pt) in [("h", None), ("l", None)]:
                src0 = ho if reg == "h" else lo
                for dc in range(8):
                    b = dc % 2
                    pt_ = pTb if b == 0 else pTb2
                    wt = [("cV", K.get(("ptbfree", b), 0))]
                    if dc == 0 and reg == "h":
                        wt += [("cV", K[("hilo", i)])] + tk_const
                    E_("t", lambda t, s0=src0, d=dc, p_=pt_: t.transpose(
                        p_[:, 0:128],
                        ar16[:, s0 + d * 128: s0 + d * 128 + 128], c_idb[:, :]),
                       wait=wt, inc="cT")
                    ct1 = N_("cT")
                    if reg == "h":
                        E_("v", lambda v, d=dc, i_=i, p_=pt_: v.tensor_copy(
                            xnh_f[:, d * TPC + i_ * 128: d * TPC + i_ * 128 + 128],
                            p_[:, 0:128]),
                           wait=[("cT", ct1)], inc="cV")
                    else:
                        E_("v", lambda v, d=dc, i_=i, p_=pt_: v.tensor_copy(
                            ar16[:, XNL + d * 2048 + i_ * 128: XNL + d * 2048 + i_ * 128 + 128],
                            p_[:, 0:128]),
                           wait=[("cT", ct1)], inc="cV")
                    K[("ptbfree", b)] = N_("cV")
                    K[("tev", i, dc, reg)] = K[("ptbfree", b)]
            K[("lntp", i)] = N_("cT")
        cv_ln = N_("cV")

        if PH <= 1:
            return _early()

        tk_agin = sc.dma("y", lambda y: y.dma_start(
            out=ag_in[:, :].rearrange("(dc p) t -> p dc t", p=128),
            in_=xnh_f[:, :].rearrange("p (dc t) -> p dc t", dc=8)),
            wait=[("cV", cv_ln)])
        E_("g", lambda g: g.collective_compute(
            "AllGather", ALU.bypass, replica_groups=GROUPS,
            ins=[ag_in[:, :]], outs=[ag_out[:, :]]),
           wait=[tk_agin], inc="dC")
        dc_ag = N_("dC")

        if PH <= 2:
            return _early()

        def VJ():
            E_("v", lambda v: v.tensor_scalar(zer8[0:8, 0:8], zer8[0:8, 0:8],
                                              1.0, None, ALU.mult), inc=None)

        SCT = 0
        nW = 8 * NTT
        for tb in range(NTB):
            toff = tb * 512
            first = True
            for (lw, rl) in [(gwt_s, "h"), (gwt_s, "l"), (gwt_lo, "h")]:
                for dc in range(8):
                    if rl == "h":
                        rap = (lambda d, to: (lambda: xnh_f[:, d * TPC + to: d * TPC + to + 512]))(dc, toff)
                    else:
                        rap = (lambda d, to: (lambda: ar16[:, XNL + d * 2048 + to: XNL + d * 2048 + to + 512]))(dc, toff)
                    last = (rl == "h" and lw is gwt_lo and dc == 7)
                    wt = []
                    if first:
                        wt = [("cV", cv_ln), ("cV", K.get(("glog", tb - 1), 0))]
                    E_("t", lambda t, lw_=lw, d=dc, ra=rap, f_=first, l_=last: t.matmul(
                        pA[0:E, 0:512], lw_[:, d * E:(d + 1) * E], ra(),
                        start=f_, stop=l_),
                       wait=wt, inc="cT" if last else None)
                    first = False
            ct_lg = N_("cT")
            E_("v", lambda v: v.tensor_copy(zb8[0:E, 0:512], pA[0:E, 0:512]),
               wait=[("cT", ct_lg)], inc="cV")
            K[("glog", tb)] = N_("cV")
            E_("s", lambda s: s.activation(zb8b[0:E, 0:512], zb8[0:E, 0:512], AF.Sigmoid),
               wait=[("cV", K[("glog", tb)])], inc="cS")
            cs_sig = N_("cS")
            for q4 in range(4):
                E_("t", lambda t, q=q4: t.transpose(
                    pB[:, q * 8:q * 8 + 8], zb8b[0:E, q * 128:(q + 1) * 128], c_idf[0:E, 0:E]),
                   wait=([("cS", cs_sig), ("cV", K.get(("gsct", tb - 1), 0))] if q4 == 0 else []),
                   inc="cT")
            ct_tp = N_("cT")
            E_("v", lambda v, tb_=tb: v.tensor_copy(
                ar32[:, SCT + tb_ * 32: SCT + tb_ * 32 + 32], pB[:, 0:32]),
               wait=[("cT", ct_tp)], inc="cV")
            K[("gsct", tb)] = N_("cV")
        cv_sct = N_("cV")
        E_("t", lambda t: t.matmul(pC[:, 0:E], c_or[0:1, 0:128], gb_row[0:1, 0:E],
                                   start=True, stop=True),
           wait=tk_const, inc="cT")
        ct_bb = N_("cT")
        BB = nW
        SFC = BB + 8
        GS = SFC + nW
        GR = GS + nW // 2
        TMP = GR + nW // 2
        TMP8 = TMP + nW // 2
        ER = TMP8 + nW
        ETMP = ER + nW
        E_("v", lambda v: v.tensor_copy(ar32[:, BB:BB + 8], pC[:, 0:E]),
           wait=[("cT", ct_bb)], inc="cV")
        VJ()
        E_("v", lambda v: v.tensor_tensor(
            ar32[:, SFC:SFC + nW], ar32[:, SCT:SCT + nW],
            ar32[:, BB:BB + 8].rearrange("p (o e) -> p o e", o=1).broadcast_to([128, NTT, 8]),
            ALU.add), wait=[("cV", cv_sct)], inc="cV")
        VJ()
        E_("v", lambda v: v.tensor_tensor(
            ar32[:, GS:GS + nW // 2],
            ar32[:, SFC:SFC + nW].rearrange("p (x two) -> p x two", two=2)[:, :, 0],
            ar32[:, SFC:SFC + nW].rearrange("p (x two) -> p x two", two=2)[:, :, 1],
            ALU.add), inc="cV")
        E_("v", lambda v: v.memset(ar32[:, GR:GR + nW // 2], 0.0), inc="cV")
        for gsel in range(NGROUP):
            E_("v", lambda v, g_=gsel: v.tensor_tensor(
                ar32[:, TMP:TMP + nW // 2],
                ar32[:, GS:GS + nW // 2].rearrange("p (i g) -> p i g", g=4)[:, :, g_:g_ + 1]
                    .broadcast_to([128, NTT, 4]),
                ar32[:, GS:GS + nW // 2].rearrange("p (i g) -> p i g", g=4),
                ALU.is_gt), inc="cV")
            VJ()
            E_("v", lambda v: v.tensor_tensor(
                ar32[:, GR:GR + nW // 2], ar32[:, GR:GR + nW // 2],
                ar32[:, TMP:TMP + nW // 2], ALU.add), inc="cV")
            if gsel < NGROUP - 1:
                nrest = NGROUP - 1 - gsel
                E_("v", lambda v, g_=gsel, nr=nrest: v.tensor_tensor(
                    ar32[:, TMP:TMP + nW // 2].rearrange("p (i g) -> p i g", g=4)[:, :, g_ + 1:4],
                    ar32[:, GS:GS + nW // 2].rearrange("p (i g) -> p i g", g=4)[:, :, g_:g_ + 1]
                        .broadcast_to([128, NTT, nr]),
                    ar32[:, GS:GS + nW // 2].rearrange("p (i g) -> p i g", g=4)[:, :, g_ + 1:4],
                    ALU.is_equal), inc="cV")
                VJ()
                E_("v", lambda v, g_=gsel: v.tensor_tensor(
                    ar32[:, GR:GR + nW // 2].rearrange("p (i g) -> p i g", g=4)[:, :, g_ + 1:4],
                    ar32[:, GR:GR + nW // 2].rearrange("p (i g) -> p i g", g=4)[:, :, g_ + 1:4],
                    ar32[:, TMP:TMP + nW // 2].rearrange("p (i g) -> p i g", g=4)[:, :, g_ + 1:4],
                    ALU.add), inc="cV")
        VJ()
        E_("v", lambda v: v.tensor_scalar(
            ar32[:, GR:GR + nW // 2], ar32[:, GR:GR + nW // 2], 2.0, None, ALU.is_lt), inc="cV")
        VJ()
        E_("v", lambda v: v.tensor_tensor(
            ar32[:, TMP8:TMP8 + nW].rearrange("p (x two) -> p x two", two=2),
            ar32[:, SFC:SFC + nW].rearrange("p (x two) -> p x two", two=2),
            ar32[:, GR:GR + nW // 2].rearrange("p (x o) -> p x o", o=1).broadcast_to([128, nW // 2, 2]),
            ALU.mult), inc="cV")
        E_("v", lambda v: v.memset(ar32[:, ER:ER + nW], 0.0), inc="cV")
        for esel in range(E):
            E_("v", lambda v, e_=esel: v.tensor_tensor(
                ar32[:, ETMP:ETMP + nW],
                ar32[:, TMP8:TMP8 + nW].rearrange("p (i e) -> p i e", e=8)[:, :, e_:e_ + 1]
                    .broadcast_to([128, NTT, 8]),
                ar32[:, TMP8:TMP8 + nW].rearrange("p (i e) -> p i e", e=8),
                ALU.is_gt), inc="cV")
            VJ()
            E_("v", lambda v: v.tensor_tensor(
                ar32[:, ER:ER + nW], ar32[:, ER:ER + nW], ar32[:, ETMP:ETMP + nW], ALU.add), inc="cV")
            if esel < E - 1:
                nrest = E - 1 - esel
                E_("v", lambda v, e_=esel, nr=nrest: v.tensor_tensor(
                    ar32[:, ETMP:ETMP + nW].rearrange("p (i e) -> p i e", e=8)[:, :, e_ + 1:8],
                    ar32[:, TMP8:TMP8 + nW].rearrange("p (i e) -> p i e", e=8)[:, :, e_:e_ + 1]
                        .broadcast_to([128, NTT, nr]),
                    ar32[:, TMP8:TMP8 + nW].rearrange("p (i e) -> p i e", e=8)[:, :, e_ + 1:8],
                    ALU.is_equal), inc="cV")
                VJ()
                E_("v", lambda v, e_=esel: v.tensor_tensor(
                    ar32[:, ER:ER + nW].rearrange("p (i e) -> p i e", e=8)[:, :, e_ + 1:8],
                    ar32[:, ER:ER + nW].rearrange("p (i e) -> p i e", e=8)[:, :, e_ + 1:8],
                    ar32[:, ETMP:ETMP + nW].rearrange("p (i e) -> p i e", e=8)[:, :, e_ + 1:8],
                    ALU.add), inc="cV")
        VJ()
        E_("v", lambda v: v.tensor_scalar(
            ar32[:, ER:ER + nW], ar32[:, ER:ER + nW], 2.0, None, ALU.is_lt), inc="cV")
        VJ()
        WSUM, WS2, WINV, WINV2, ETMP2 = 1792, 1808, 1824, 1840, 1856
        E_("v", lambda v: v.tensor_tensor(
            ar32[:, ETMP:ETMP + nW], ar32[:, SCT:SCT + nW], ar32[:, ER:ER + nW], ALU.mult), inc="cV")
        cv_et = N_("cV")
        E_("s", lambda s: s.activation(ar32[:, ETMP2:ETMP2 + nW], ar32[:, ETMP:ETMP + nW],
                                       AF.Copy), wait=[("cV", cv_et)], inc="cS")
        cs_et2 = N_("cS")
        for i in range(NTT):
            E_("v", lambda v, i_=i: v.reduce_sum(
                ar32[:, WSUM + i_:WSUM + i_ + 1],
                ar32[:, ETMP2 + i_ * 8:ETMP2 + i_ * 8 + 8], AXX),
               wait=([("cS", cs_et2)] if i == 0 else []), inc="cV")
        cv_ws = N_("cV")
        E_("s", lambda s: s.activation(ar32[:, WS2:WS2 + NTT], ar32[:, WSUM:WSUM + NTT],
                                       AF.Ln, bias=c_eps[:, 4:5], scale=1.0),
           wait=[("cV", cv_ws)], inc="cS")
        E_("s", lambda s: s.activation(ar32[:, WINV2:WINV2 + NTT], ar32[:, WS2:WS2 + NTT],
                                       AF.Exp, bias=0.0, scale=-1.0), inc="cS")
        cs_wi = N_("cS")
        for i in range(NTT):
            E_("v", lambda v, i_=i: v.tensor_scalar(
                cw_all[:, i_ * 8:(i_ + 1) * 8], ar32[:, ETMP2 + i_ * 8:ETMP2 + i_ * 8 + 8],
                ar32[:, WINV2 + i_:WINV2 + i_ + 1], None, ALU.mult),
               wait=([("cS", cs_wi)] if i == 0 else []), inc="cV")
        cv_gate = N_("cV")

        if PH <= 3:
            return _early()

        def emit_wload(ee, w):
            wb_ = (ee % 2) * 12288
            return toks([
                sc.dma("g", lambda g, e_=ee, b_=wb_: g.dma_start(
                    out=ar16[:, b_:b_ + 4096].rearrange("p (hc i) -> p hc i", hc=8),
                    in_=gp[e_, :, :].rearrange("(hc p) i -> p hc i", p=128)), wait=w),
                sc.dma("g", lambda g, e_=ee, b_=wb_: g.dma_start(
                    out=ar16[:, b_ + 4096:b_ + 8192].rearrange("p (hc i) -> p hc i", hc=8),
                    in_=up[e_, :, :].rearrange("(hc p) i -> p hc i", p=128))),
                sc.dma("g", lambda g, e_=ee, b_=wb_: g.dma_start(
                    out=ar16[:, b_ + 8192:b_ + 12288].rearrange("p (ic j) -> p ic j", ic=4),
                    in_=dp[e_, :, :].rearrange("(ic p) j -> p ic j", p=128)))])

        tk_moew = [emit_wload(0, [("cV", cv_gate)]),
                   emit_wload(1, [("cV", cv_gate)])]
        PRD, GUE = 24576, 28672
        prodslots = {}
        tok_dD = {}
        flat = 0
        prev_pa = 0
        prev_pb = 0
        for e in range(E):
            wb = (e % 2) * 12288
            if (DBG & 4) and e == 1:
                tok_dC = sc.dma("y", lambda y: y.dma_start(
                    out=dbgC[:, :], in_=acc[:, 0:NTT * 1024]),
                    wait=[("cV", K[("moeacc", 0, NTT - 1, 0)]),
                          ("cV", K[("moeacc", 0, NTT - 1, 1)])])
            for tb in range(NTB):
                toff = tb * 512
                pb = PRD + (flat % 2) * 2048
                if (DBG & 2) and flat == 2:
                    tok_dB = sc.dma("y", lambda y: y.dma_start(
                        out=dbgB[:, :], in_=ar16[:, PRD:PRD + 2048]),
                        wait=[("cV", K[("moeprodv", 0, 3)])])
                cv_prod = 0
                for ic in range(4):
                    wmm = []
                    if tb == 0 and ic == 0:
                        wmm = tk_moew[e] + [("cV", cv_gate)]
                    wmm.append(("cS", prev_pa))
                    if ic == 0 and flat >= 2:
                        wmm.append(("cT", prodslots[flat - 2]))
                    if (DBG & 2) and flat == 2 and ic == 0:
                        wmm.append(tok_dB)
                    for dc in range(8):
                        E_("t", lambda t, h_=dc, i_=ic, wb_=wb, to=toff: t.matmul(
                            pA[:, 0:512],
                            ar16[:, wb_ + h_ * 512 + i_ * 128: wb_ + h_ * 512 + i_ * 128 + 128],
                            xnh_f[:, h_ * TPC + to: h_ * TPC + to + 512],
                            start=(h_ == 0), stop=(h_ == 7)),
                           wait=(wmm if dc == 0 else []), inc="cT" if dc == 7 else None)
                    ct_g = N_("cT")
                    for dc in range(8):
                        E_("t", lambda t, h_=dc, i_=ic, wb_=wb, to=toff: t.matmul(
                            pB[:, 0:512],
                            ar16[:, wb_ + 4096 + h_ * 512 + i_ * 128: wb_ + 4096 + h_ * 512 + i_ * 128 + 128],
                            xnh_f[:, h_ * TPC + to: h_ * TPC + to + 512],
                            start=(h_ == 0), stop=(h_ == 7)),
                           wait=([("cV", prev_pb)] if dc == 0 else []),
                           inc="cT" if dc == 7 else None)
                    ct_u = N_("cT")
                    E_("s", lambda s: s.activation(esk[:, 0:512], pA[:, 0:512], AF.Silu),
                       wait=[("cT", ct_g),
                             ("cV", K.get(("moeprodv", flat, ic - 1),
                                          K.get(("moeprodv", flat - 1, 3), 0)))], inc="cS")
                    prev_pa = N_("cS")
                    E_("v", lambda v: v.tensor_copy(ar16[:, GUE + 512:GUE + 1024], pB[:, 0:512]),
                       wait=[("cT", ct_u)], inc="cV")
                    prev_pb = N_("cV")
                    E_("v", lambda v, p_=pb, i_=ic: v.tensor_tensor(
                        ar16[:, p_ + i_ * 512: p_ + (i_ + 1) * 512],
                        esk[:, 0:512], ar16[:, GUE + 512:GUE + 1024], ALU.mult),
                       wait=[("cS", prev_pa)], inc="cV")
                    cv_prod = N_("cV")
                    K[("moeprodv", flat, ic)] = cv_prod
                for q4 in range(4):
                    gtile = tb * 4 + q4
                    for half in range(2):
                        pd_ = pC if half == 0 else pD
                        wdn = []
                        if q4 == 0 and half == 0:
                            wdn.append(("cV", cv_prod))
                        wdn.append(("cV", K.get(("moeacc", e, gtile - 1, half),
                                               K.get(("moeaccprev", e, 1), 0))))
                        if (DBG & 4) and e == 1 and gtile == 0:
                            wdn.append(tok_dC)
                        if (DBG & 24) and e >= 1 and gtile == 0 and (e - 1) in tok_dD:
                            wdn.append(tok_dD[e - 1])
                        for ic in range(4):
                            E_("t", lambda t, i_=ic, q_=q4, h_=half, wb_=wb, p_=pb, po=pd_: t.matmul(
                                po[:, 0:512],
                                ar16[:, p_ + i_ * 512 + q_ * 128: p_ + i_ * 512 + q_ * 128 + 128],
                                ar16[:, wb_ + 8192 + i_ * 1024 + h_ * 512: wb_ + 8192 + i_ * 1024 + (h_ + 1) * 512],
                                start=(i_ == 0), stop=(i_ == 3)),
                               wait=(wdn if ic == 0 else []), inc="cT" if ic == 3 else None)
                        ct_d = N_("cT")
                        E_("v", lambda v, h_=half, g_=gtile, ee=e, po=pd_: v.scalar_tensor_tensor(
                            acc[:, g_ * 1024 + h_ * 512: g_ * 1024 + (h_ + 1) * 512],
                            po[:, 0:512],
                            cw_all[:, g_ * 8 + ee: g_ * 8 + ee + 1],
                            acc[:, g_ * 1024 + h_ * 512: g_ * 1024 + (h_ + 1) * 512],
                            ALU.mult, ALU.add),
                           wait=[("cT", ct_d)], inc="cV")
                        K[("moeacc", e, gtile, half)] = N_("cV")
                prodslots[flat] = N_("cT")
                flat += 1
            K[("moeaccprev", e + 1, 0)] = K[("moeacc", e, NTT - 1, 0)]
            K[("moeaccprev", e + 1, 1)] = K[("moeacc", e, NTT - 1, 1)]
            K[("moednlast", e)] = N_("cT")
            if DBG & 8:
                tok_dD[e] = sc.dma("y", lambda y, e_=e: y.dma_start(
                    out=dbgD[e_ * 128:(e_ + 1) * 128, :], in_=acc[:, 0:NTT * 1024]),
                    wait=[("cV", K[("moeacc", e, NTT - 1, 0)]),
                          ("cV", K[("moeacc", e, NTT - 1, 1)])])
            elif DBG & 16:
                tok_dD[e] = sc.dma("y", lambda y, e_=e: y.dma_start(
                    out=dbgD[e_ * 128:(e_ + 1) * 128, 0:8], in_=acc[:, 0:8]),
                    wait=[("cV", K[("moeacc", e, NTT - 1, 0)]),
                          ("cV", K[("moeacc", e, NTT - 1, 1)])])
            if e + 2 < E:
                w = [("cT", K[("moednlast", e)])]
                if (DBG & 1) and e + 2 == 2:
                    tok_dA = sc.dma("y", lambda y: y.dma_start(
                        out=dbgA[:, :], in_=ar16[:, 0:12288]), wait=tk_moew[0])
                    w.append(tok_dA)
                tk_moew.append(emit_wload(e + 2, w))
        cv_moe = N_("cV")
        ct_moe = N_("cT")

        if PH <= 4:
            return _early()

        XNP = 0
        for sb in range(SEQ // 512):
            blk = (sb * 512) // TPC
            scol = (sb * 512) % TPC
            xb = XNP + (sb % 2) * 4096
            w = [("dC", dc_ag), ("cT", ct_moe)]
            if sb >= 2:
                w.append(("cT", K.get(("p5t", sb - 2), 0)))
            tk5 = sc.dma("y", lambda y, b_=blk, s_=scol, x_=xb: y.dma_start(
                out=ar16[:, x_:x_ + 4096].rearrange("p (dc t) -> p dc t", dc=8),
                in_=ag_out[b_ * 1024:(b_ + 1) * 1024, s_:s_ + 512]
                    .rearrange("(dc p) t -> p dc t", p=128)), wait=w)
            for dc in range(8):
                wt = []
                if dc == 0:
                    wt = [tk5] + tk_attnw + [("cV", K.get(("p5ev", sb - 1), cv_moe))]
                E_("t", lambda t, d=dc, x_=xb: t.matmul(
                    pB[0:8, 0:512], wbf_s[:, d * 8:(d + 1) * 8],
                    ar16[:, x_ + d * 512: x_ + (d + 1) * 512],
                    start=(d == 0), stop=(d == 7)),
                   wait=wt, inc="cT" if dc == 7 else None)
            K[("p5t", sb)] = N_("cT")
            ct_z = K[("p5t", sb)]
            E_("v", lambda v: v.tensor_copy(zb8[0:8, 0:512], pB[0:8, 0:512]),
               wait=[("cT", ct_z)], inc="cV")
            K[("p5ev", sb)] = N_("cV")
            E_("s", lambda s: s.activation(zb8b[0:8, 0:512], zb8[0:8, 0:512], AF.Sigmoid),
               wait=[("cV", K[("p5ev", sb)])], inc="cS")
            E_("s", lambda s: s.activation(zb8b[0:8, 0:512], zb8b[0:8, 0:512], AF.Ln), inc="cS")
            cs_ln2 = N_("cS")
            E_("v", lambda v: v.tensor_scalar(
                zb8[0:8, 0:512], zb8b[0:8, 0:512], LOG2, None, ALU.add),
               wait=[("cS", cs_ln2)], inc="cV")
            for sg in range(4):
                E_("v", lambda v, s_=sg: v.tensor_tensor_scan(
                    zb8[0:4, s_ * 128:(s_ + 1) * 128], zb8b[0:4, s_ * 128:(s_ + 1) * 128],
                    zer8[0:4, 0:128], 0.0, ALU.add, ALU.add), inc="cV")
            cv_sc2 = N_("cV")
            for sg in range(4):
                jj = sb * 4 + sg
                pt_ = pC if sg % 2 == 0 else pD
                E_("t", lambda t, s_=sg, p_=pt_: t.transpose(
                    p_[:, 0:8], zb8[0:8, s_ * 128:(s_ + 1) * 128],
                    c_idf[0:8, 0:8]),
                   wait=([("cV", cv_sc2)] if sg == 0 else
                         [("cV", K.get(("lgev", sb, sg - 2), 0))]),
                   inc="cT")
                ctt = N_("cT")
                E_("v", lambda v, j_=jj, p_=pt_: v.tensor_copy(
                    lblgT[:, j_ * 8:(j_ + 1) * 8], p_[:, 0:8]),
                   wait=[("cT", ctt)], inc="cV")
                K[("lgev", sb, sg)] = N_("cV")
        cv_lblg = N_("cV")
        tk_rows = toks([
            sc.dma("g", lambda g: g.dma_start(
                out=mrow[0:1, :],
                in_=lblgT[63:64, :].rearrange("p (j x) -> p j x", x=8)[:, :, 0:4]),
                wait=[("cV", cv_lblg)]),
            sc.dma("g", lambda g: g.dma_start(
                out=lgcrow[0:1, :],
                in_=lblgT[127:128, :].rearrange("p (j x) -> p j x", x=8)[:, :, 0:4]))])
        E_("v", lambda v: v.memset(mnrow[0:1, :], 0.0), wait=tk_rows, inc="cV")
        E_("v", lambda v: v.tensor_copy(mnrow[0:1, 0:4 * NCH - 4], mrow[0:1, 4:4 * NCH]), inc="cV")
        E_("v", lambda v: v.tensor_tensor(
            d1a[:, :], lblgT[:, :].rearrange("p (j x) -> p j x", x=8)[:, :, 4:8],
            lblgT[:, :].rearrange("p (j x) -> p j x", x=8)[:, :, 0:4], ALU.subtract), inc="cV")
        VJ()
        cv_rows = N_("cV")
        E_("t", lambda t: t.matmul(pC[:, 0:4 * NCH], c_or[0:1, 0:128], mrow[0:1, :],
                                   start=True, stop=True), wait=[("cV", cv_rows)], inc="cT")
        ct_mb = N_("cT")
        E_("v", lambda v: v.tensor_tensor(swk[:, :], pC[:, 0:4 * NCH], d1a[:, :], ALU.add),
           wait=[("cT", ct_mb)], inc="cV")
        cv1 = N_("cV")
        E_("s", lambda s: s.activation(sbar[:, :], swk[:, :], AF.Exp),
           wait=[("cV", cv1)], inc="cS")
        cs1 = N_("cS")
        E_("v", lambda v: v.tensor_tensor(
            swk[:, :], lblgT[:, :].rearrange("p (j x) -> p j x", x=8)[:, :, 0:4],
            pC[:, 0:4 * NCH], ALU.subtract), wait=[("cS", cs1)], inc="cV")
        cv2 = N_("cV")
        E_("s", lambda s: s.activation(gga[:, :], swk[:, :], AF.Exp),
           wait=[("cV", cv2)], inc="cS")
        cs_gg = N_("cS")
        E_("v", lambda v: v.tensor_tensor(zzrow[0:1, :], mnrow[0:1, :], lgcrow[0:1, :], ALU.add),
           wait=[("cS", cs_gg)], inc="cV")
        cv_zz = N_("cV")
        E_("t", lambda t: t.matmul(pC[:, 256:256 + 4 * NCH], c_or[0:1, 0:128], zzrow[0:1, :],
                                   start=True, stop=True), wait=[("cV", cv_zz)], inc="cT")
        ct_zb2 = N_("cT")
        E_("v", lambda v: v.tensor_tensor(swk[:, :], pC[:, 256:256 + 4 * NCH], d1a[:, :], ALU.add),
           wait=[("cT", ct_zb2)], inc="cV")
        cv3 = N_("cV")
        E_("s", lambda s: s.activation(shk[:, :], swk[:, :], AF.Exp),
           wait=[("cV", cv3)], inc="cS")
        cs_shk = N_("cS")
        E_("v", lambda v: v.tensor_tensor(zzrow[0:1, :], zzrow[0:1, :], mrow[0:1, :], ALU.subtract),
           wait=[("cS", cs_shk)], inc="cV")
        cv_zz2 = N_("cV")
        E_("t", lambda t: t.matmul(pD[:, 0:4 * NCH], c_or[0:1, 0:128], zzrow[0:1, :],
                                   start=True, stop=True), wait=[("cV", cv_zz2)], inc="cT")
        ct_dk = N_("cT")
        E_("s", lambda s: s.activation(dkb[:, :], pD[:, 0:4 * NCH], AF.Exp),
           wait=[("cT", ct_dk)], inc="cS")
        cs_p5 = N_("cS")
        E_("v", lambda v: v.memset(ssf[:, :], 0.0), wait=[("cS", cs_p5)], inc="cV")
        E_("v", lambda v: v.memset(ssb[:, :], 0.0), inc="cV")
        cv_p5 = N_("cV")

        if PH <= 5:
            return _early()

        NGRP = NCH // 4
        for grp in range(NGRP):
            for jj in range(4):
                j = grp * 4 + jj
                blk = (j * C) // TPC
                scol = (j * C) % TPC
                xb = (j % 2) * 1024
                slot = j % 4
                w = [("dC", dc_ag), ("cS", cs_p5)]
                if j >= 2:
                    w.append(("cT", K.get(("dnproj", j - 2), 0)))
                tkxg = sc.dma("y", lambda y, b_=blk, s_=scol, x_=xb: y.dma_start(
                    out=xng[:, x_:x_ + 1024].rearrange("p (dc t) -> p dc t", dc=8),
                    in_=ag_out[b_ * 1024:(b_ + 1) * 1024, s_:s_ + 128]
                        .rearrange("(dc p) t -> p dc t", p=128)), wait=w)
                for dc in range(8):
                    wt = []
                    if dc == 0:
                        wt = [tkxg, ("cV", K.get(("kvev", j - 1), cv_p5)),
                              ("cS", K.get(("kex", j - 1), cs_p5))]
                    E_("t", lambda t, d=dc, x_=xb: t.matmul(
                        pA[:, 0:512], xng[:, x_ + d * 128: x_ + (d + 1) * 128],
                        wkv_s[:, d * 512:(d + 1) * 512],
                        start=(d == 0), stop=(d == 7)),
                       wait=wt, inc="cT" if dc == 7 else None)
                ct_kv = N_("cT")
                for ms in range(2):
                    for dc in range(8):
                        wt = []
                        if ms == 0 and dc == 0:
                            wt = [("cV", K.get(("qfev", j - 1), cv_p5)),
                                  ("cS", K.get(("qex", j - 1), cs_p5))]
                        E_("t", lambda t, d=dc, m_=ms, x_=xb: t.matmul(
                            pB[:, m_ * 128:(m_ + 1) * 128],
                            wq_s[:, d * 256 + m_ * 128: d * 256 + (m_ + 1) * 128],
                            xng[:, x_ + d * 128: x_ + (d + 1) * 128],
                            start=(d == 0), stop=(d == 7)),
                           wait=wt, inc="cT" if dc == 7 else None)
                ct_qf = N_("cT")
                K[("dnproj", j)] = ct_qf
                E_("s", lambda s: s.activation(esk[:, 0:256], pA[:, 0:256], AF.Exp,
                                               bias=0.0, scale=-1.0),
                   wait=[("cT", ct_kv)], inc="cS")
                K[("kex", j)] = N_("cS")
                E_("s", lambda s: s.activation(esk[:, 256:512], pB[:, 0:256], AF.Exp,
                                               bias=0.0, scale=-1.0),
                   wait=[("cT", ct_qf)], inc="cS")
                K[("qex", j)] = N_("cS")
                E_("v", lambda v: v.tensor_scalar(esk[:, 0:256], esk[:, 0:256], 1.0, None, ALU.add),
                   wait=[("cS", K[("kex", j)])], inc="cV")
                E_("v", lambda v: v.reciprocal(esk[:, 0:256], esk[:, 0:256]), inc="cV")
                E_("v", lambda v: v.tensor_tensor(kmf[:, :], pA[:, 0:256], esk[:, 0:256], ALU.mult),
                   inc="cV")
                cv_kmf = N_("cV")
                wq4 = [("cS", K[("qex", j)])]
                if j >= 4:
                    wq4.append(("cT", K.get(("omm", j - 4, 3), 0)))
                E_("v", lambda v: v.tensor_scalar(esk[:, 256:512], esk[:, 256:512], 1.0, None, ALU.add),
                   wait=wq4, inc="cV")
                E_("v", lambda v: v.reciprocal(esk[:, 256:512], esk[:, 256:512]), inc="cV")
                E_("v", lambda v, sl=slot: v.tensor_tensor(
                    qft[:, sl * 256:(sl + 1) * 256], pB[:, 0:256], esk[:, 256:512], ALU.mult),
                   inc="cV")
                K[("qfev", j)] = N_("cV")
                wv4 = []
                if j >= 4:
                    wv4.append(("cT", K.get(("zlast", j - 4), 0)))
                E_("v", lambda v, sl=slot: v.tensor_copy(
                    kvt[:, sl * 512 + 256: sl * 512 + 512], pA[:, 256:512]),
                   wait=wv4, inc="cV")
                K[("kvev", j)] = N_("cV")
                E_("v", lambda v: v.tensor_tensor(ksq[:, :], kmf[:, :], kmf[:, :], ALU.mult),
                   wait=[("cV", cv_kmf)], inc="cV")
                for hh in range(4):
                    E_("v", lambda v, h_=hh: v.reduce_sum(
                        knc[:, h_:h_ + 1], ksq[:, h_ * 64:(h_ + 1) * 64], AXX), inc="cV")
                cv_ks = N_("cV")
                E_("s", lambda s: s.activation(knc[:, 4:8], knc[:, 0:4], AF.Ln,
                                               bias=c_eps[:, 1:2], scale=1.0),
                   wait=[("cV", cv_ks)], inc="cS")
                E_("s", lambda s: s.activation(knc[:, 8:12], knc[:, 4:8], AF.Exp,
                                               bias=0.0, scale=-0.5), inc="cS")
                cs_kn = N_("cS")
                E_("v", lambda v, j_=j: v.tensor_tensor(
                    knc[:, 12:16], knc[:, 8:12], gga[:, j_ * 4:(j_ + 1) * 4], ALU.mult),
                   wait=[("cS", cs_kn)], inc="cV")
                E_("v", lambda v, j_=j: v.tensor_tensor(
                    knc[:, 4:8], knc[:, 8:12], shk[:, j_ * 4:(j_ + 1) * 4], ALU.mult), inc="cV")
                wk4 = []
                if j >= 4:
                    wk4.append(("cT", K.get(("supmm", j - 4, 3), 0)))
                    wk4.append(("cT", K.get(("kftp", j - 4), 0)))
                for hh in range(4):
                    E_("v", lambda v, h_=hh, sl=slot: v.tensor_scalar(
                        kvt[:, sl * 512 + h_ * 64: sl * 512 + (h_ + 1) * 64],
                        kmf[:, h_ * 64:(h_ + 1) * 64], knc[:, 12 + h_:13 + h_], None, ALU.mult),
                       wait=(wk4 if hh == 0 else []), inc="cV")
                    E_("v", lambda v, h_=hh, sl=slot: v.tensor_scalar(
                        kck[:, sl * 256 + h_ * 64: sl * 256 + (h_ + 1) * 64],
                        kmf[:, h_ * 64:(h_ + 1) * 64], knc[:, 4 + h_:5 + h_], None, ALU.mult),
                       inc="cV")
                    E_("v", lambda v, h_=hh, sl=slot: v.tensor_scalar(
                        knm[:, sl * 256 + h_ * 64: sl * 256 + (h_ + 1) * 64],
                        kmf[:, h_ * 64:(h_ + 1) * 64], knc[:, 8 + h_:9 + h_], None, ALU.mult),
                       inc="cV")
                cv_scl = N_("cV")
                for kk in range(2):
                    wt = []
                    if kk == 0:
                        wt = [("cV", cv_scl), ("cV", K.get(("kfev", j - 1), 0))]
                    E_("t", lambda t, k_=kk, sl=slot: t.transpose(
                        pTb[:, k_ * 128:(k_ + 1) * 128],
                        knm[:, sl * 256 + k_ * 128: sl * 256 + (k_ + 1) * 128], c_idb[:, :]),
                       wait=wt, inc="cT")
                K[("kftp", j)] = N_("cT")
                wkf = [("cT", K[("kftp", j)])]
                if j >= 4:
                    wkf.append(("cT", K.get(("omm", j - 4, 3), 0)))
                E_("v", lambda v, sl=slot: v.tensor_copy(kft[:, sl * 256:(sl + 1) * 256], pTb[:, 0:256]),
                   wait=wkf, inc="cV")
                K[("kfev", j)] = N_("cV")

            for jj in range(4):
                j = grp * 4 + jj
                slot = j % 4
                for h in range(HPC):
                    par = (h % 2) * 64
                    qsl = (h // 2) * 128
                    prev_ea = K.get(("eaex", j, h - 1), K.get(("eaex", j - 1, 3), 0))
                    E_("v", lambda v, j_=j, h_=h: v.tensor_scalar(
                        cbt[:, :], c_m1[:, :], d1a[:, j_ * 4 + h_:j_ * 4 + h_ + 1], None, ALU.add),
                       wait=[("cV", K[("kfev", j)]), ("cS", prev_ea)], inc="cV")
                    cv_cb1 = N_("cV")
                    E_("t", lambda t: t.transpose(pC[:, 0:128], cbt[:, :], c_idf[:, :]),
                       wait=[("cV", cv_cb1)], inc="cT")
                    ct_c1 = N_("cT")
                    E_("s", lambda s, j_=j, h_=h: s.activation(
                        ea_t[:, 0:128], pC[:, 0:128], AF.Exp,
                        bias=lblgT[:, j_ * 8 + h_:j_ * 8 + h_ + 1], scale=1.0),
                       wait=[("cT", ct_c1)], inc="cS")
                    cs_e1 = N_("cS")
                    E_("v", lambda v, j_=j, h_=h: v.tensor_scalar(
                        cbt[:, :], c_m2[:, :], lblgT[:, j_ * 8 + h_:j_ * 8 + h_ + 1],
                        None, ALU.add), wait=[("cS", cs_e1)], inc="cV")
                    cv_cb2 = N_("cV")
                    E_("t", lambda t: t.transpose(pC[:, 128:256], cbt[:, :], c_idf[:, :]),
                       wait=[("cV", cv_cb2)], inc="cT")
                    ct_c2 = N_("cT")
                    E_("s", lambda s, j_=j, h_=h: s.activation(
                        ea_t[:, 128:256], pC[:, 128:256], AF.Exp,
                        bias=d1a[:, j_ * 4 + h_:j_ * 4 + h_ + 1], scale=1.0),
                       wait=[("cT", ct_c2)], inc="cS")
                    K[("eaex", j, h)] = N_("cS")
                    cs_e2 = K[("eaex", j, h)]
                    wg = [("cV", K.get(("xxev", j, h - 1), K[("kfev", j)]))]
                    if h == 0:
                        wg.append(("cV", K.get(("msev", j - 1), 0)))
                    E_("t", lambda t, sl=slot, p_=par, q_=qsl: t.matmul(
                        pD[:, 0:128],
                        kft[p_:p_ + 64, sl * 256 + q_: sl * 256 + q_ + 128],
                        kft[p_:p_ + 64, sl * 256 + q_: sl * 256 + q_ + 128],
                        start=True, stop=True),
                       wait=wg, inc="cT")
                    E_("t", lambda t, sl=slot, p_=par, q_=qsl: t.matmul(
                        pD[:, 128:256],
                        kft[p_:p_ + 64, sl * 256 + q_: sl * 256 + q_ + 128],
                        qft[p_:p_ + 64, sl * 256 + q_: sl * 256 + q_ + 128],
                        start=True, stop=True), inc="cT")
                    ct_g = N_("cT")
                    E_("v", lambda v: v.tensor_tensor(xx[:, :], pD[:, 0:128], ea_t[:, 0:128], ALU.mult),
                       wait=[("cT", ct_g), ("cS", cs_e2)], inc="cV")
                    E_("v", lambda v: v.tensor_tensor(xxt[:, :], pD[:, 0:128], ea_t[:, 128:256],
                                                      ALU.mult), inc="cV")
                    E_("v", lambda v, j_=j, h_=h: v.scalar_tensor_tensor(
                        mt[:, :], pD[:, 128:256], sbar[:, j_ * 4 + h_:j_ * 4 + h_ + 1],
                        c_incl[:, :], ALU.mult, ALU.mult), inc="cV")
                    cv_xe = N_("cV")
                    K[("xxev", j, h)] = cv_xe
                    sq_ev = []
                    for (si, (la_, ra_, dst)) in enumerate([
                            (xxt, xx, a2), (xx, xxt, a2t), (a2t, a2, a4),
                            (a2, a2t, a4t), (a4, a4t, a8t)]):
                        sqp = pE if si % 2 == 0 else pD
                        E_("t", lambda t, l_=la_, r_=ra_, p_=sqp: t.matmul(
                            p_[:, 0:128], l_[:, :], r_[:, :],
                            start=True, stop=True),
                           wait=[("cV", cv_xe if si < 2 else sq_ev[si - 2])], inc="cT")
                        cts = N_("cT")
                        E_("v", lambda v, d_=dst, p_=sqp: v.tensor_copy(
                            d_[:, :], p_[:, 0:128]),
                           wait=[("cT", cts)], inc="cV")
                        sq_ev.append(N_("cV"))
                    K[("xxev", j, h)] = sq_ev[3]
                    rap = (lambda sl, h_: (lambda: kvt[:, sl * 512:(sl + 1) * 512]
                                           .rearrange("p (x b c) -> p x b c", x=2, b=4)[:, :, h_, :]))(slot, h)
                    zprev = 0
                    for (si, (la_, rin, dst)) in enumerate([
                            (a8t, None, zz1), (a4t, zz1, zz2), (a2t, zz2, zz1), (xxt, zz1, zz2)]):
                        E_("t", lambda t, l_=la_, r_=rin, s_=si, ra=rap: t.matmul(
                            pE[:, (s_ % 4) * 128:(s_ % 4) * 128 + 128], l_[:, :],
                            (ra() if r_ is None else r_[:, :]),
                            start=True, stop=True),
                           wait=[("cV", sq_ev[4] if si == 0 else zprev)], inc="cT")
                        ctz = N_("cT")
                        if si == 0:
                            K[("zfirst", j, h)] = ctz
                        op = ALU.add if si < 3 else ALU.subtract
                        if si < 3:
                            E_("v", lambda v, d_=dst, r_=rin, s_=si, ra=rap, o_=op: v.tensor_tensor(
                                d_[:, :], (ra() if r_ is None else r_[:, :]),
                                pE[:, (s_ % 4) * 128:(s_ % 4) * 128 + 128], o_),
                               wait=[("cT", ctz)], inc="cV")
                        else:
                            E_("v", lambda v, d_=dst, r_=rin, s_=si: v.tensor_tensor(
                                d_[:, :], r_[:, :],
                                pE[:, (s_ % 4) * 128:(s_ % 4) * 128 + 128], ALU.subtract),
                               wait=[("cT", ctz)], inc="cV")
                        zprev = N_("cV")
                        if si == 3:
                            K[("zlastmm", j, h)] = ctz
                    cv_z = zprev
                    wwf = [("cV", cv_z)]
                    if h == 0:
                        wwf.append(("cV", K.get(("ontev", j - 1, 3), 0)))
                    E_("t", lambda t, p_=par: t.transpose(
                        pTb2[p_:p_ + 64, 0:128], zz2[:, 0:64], c_idb[:, :],
                        tile_position=(0, p_)),
                       wait=wwf, inc="cT")
                    ct_wf = N_("cT")
                    E_("v", lambda v, p_=par: v.tensor_copy(
                        wfb[p_:p_ + 64, 0:128], pTb2[p_:p_ + 64, 0:128]),
                       wait=[("cT", ct_wf)], inc="cV")
                    cv_wf = N_("cV")
                    E_("t", lambda t, p_=par, h_=h: t.matmul(
                        pE[:, 128:192], wfb[p_:p_ + 64, 0:128],
                        ssb[p_:p_ + 64, (h_ // 2) * 64:(h_ // 2) * 64 + 64],
                        start=True, stop=True),
                       wait=[("cV", cv_wf)], inc="cT")
                    ct_u = N_("cT")
                    E_("v", lambda v: v.tensor_tensor(uu[:, :], zz2[:, 64:128], pE[:, 128:192],
                                                      ALU.subtract),
                       wait=[("cT", ct_u)], inc="cV")
                    cv_u = N_("cV")
                    E_("t", lambda t, h_=h: t.matmul(
                        pF[0:64, h_ * 128:(h_ + 1) * 128], uu[:, :], mt[:, :],
                        start=True, stop=False),
                       wait=[("cV", cv_u)], inc=None)
                    E_("t", lambda t, h_=h, p_=par, q_=qsl, sl=slot: t.matmul(
                        pF[0:64, h_ * 128:(h_ + 1) * 128],
                        ssb[p_:p_ + 64, (h_ // 2) * 64:(h_ // 2) * 64 + 64],
                        qft[p_:p_ + 64, sl * 256 + q_: sl * 256 + q_ + 128],
                        start=False, stop=True), inc="cT")
                    K[("omm", j, h)] = N_("cT")
                    E_("t", lambda t, h_=h, p_=par, sl=slot: t.matmul(
                        pE[p_:p_ + 64, 256 + (h_ // 2) * 64:256 + (h_ // 2) * 64 + 64],
                        kck[:, sl * 256 + h_ * 64: sl * 256 + (h_ + 1) * 64], uu[:, :],
                        start=True, stop=True,
                        tile_position=(0, p_)),
                       inc="cT")
                    K[("supmm", j, h)] = N_("cT")
                    ct_s = K[("supmm", j, h)]
                    E_("v", lambda v, h_=h, p_=par, j_=j: v.scalar_tensor_tensor(
                        ssf[p_:p_ + 64, (h_ // 2) * 64:(h_ // 2) * 64 + 64],
                        ssf[p_:p_ + 64, (h_ // 2) * 64:(h_ // 2) * 64 + 64],
                        dkb[p_:p_ + 64, j_ * 4 + h_:j_ * 4 + h_ + 1],
                        pE[p_:p_ + 64, 256 + (h_ // 2) * 64:256 + (h_ // 2) * 64 + 64],
                        ALU.mult, ALU.add),
                       wait=[("cT", ct_s)], inc="cV")
                    VJ()
                    E_("v", lambda v, h_=h, p_=par: v.tensor_copy(
                        ssb[p_:p_ + 64, (h_ // 2) * 64:(h_ // 2) * 64 + 64],
                        ssf[p_:p_ + 64, (h_ // 2) * 64:(h_ // 2) * 64 + 64]), inc="cV")
                    K[("sup", j, h)] = N_("cV")
                K[("zlast", j)] = K[("zlastmm", j, 3)]
                ct_o_all = K[("omm", j, 3)]
                E_("v", lambda v: v.tensor_copy(offb[:, :], pF[0:64, 0:512]),
                   wait=[("cT", ct_o_all)], inc="cV")
                cv_off = N_("cV")
                oslot = (j % 2) * 256
                for h in range(HPC):
                    wt = []
                    if h == 0:
                        wt = [("cV", cv_off), ("cV", K.get(("ontev", j - 1, 3), 0))]
                    E_("t", lambda t, h_=h: t.transpose(
                        pTb2[:, 256 + h_ * 64:256 + (h_ + 1) * 64],
                        offb[:, h_ * 128:(h_ + 1) * 128], c_idb[0:64, 0:64]),
                       wait=wt, inc="cT")
                ct_ot = N_("cT")
                for h in range(HPC):
                    wt = [("cT", ct_ot)] if h == 0 else []
                    if h == 0 and j >= 2:
                        wt.append(K[("odma", j - 2)])
                    E_("v", lambda v, h_=h, os_=oslot, j_=j: v.tensor_scalar(
                        ont[:, os_ + h_ * 64: os_ + (h_ + 1) * 64],
                        pTb2[:, 256 + h_ * 64:256 + (h_ + 1) * 64],
                        gga[:, j_ * 4 + h_:j_ * 4 + h_ + 1], None, ALU.mult),
                       wait=wt, inc="cV")
                    K[("ontev", j, h)] = N_("cV")
                E_("s", lambda s, os_=oslot: s.activation(osq[0:128, 0:256] if False else
                                                          ar32[:, 2560:2816],
                                                          ont[:, os_:os_ + 256], AF.Square),
                   wait=[("cV", K[("ontev", j, 3)])], inc="cS")
                cs_sq2 = N_("cS")
                for h in range(HPC):
                    E_("v", lambda v, h_=h: v.reduce_sum(
                        msc[:, h_:h_ + 1], ar32[:, 2560 + h_ * 64:2560 + (h_ + 1) * 64], AXX),
                       wait=([("cS", cs_sq2)] if h == 0 else []), inc="cV")
                K[("msev", j)] = N_("cV")
                E_("s", lambda s: s.activation(msc[:, 8:12], msc[:, 0:4], AF.Ln,
                                               bias=c_eps[:, 2:3], scale=1.0 / HD),
                   wait=[("cV", K[("msev", j)])], inc="cS")
                E_("s", lambda s: s.activation(msc[:, 4:8], msc[:, 8:12], AF.Exp,
                                               bias=0.0, scale=-0.5), inc="cS")
                cs_c = N_("cS")
                for h in range(HPC):
                    E_("v", lambda v, h_=h, os_=oslot: v.tensor_scalar(
                        ont[:, os_ + h_ * 64: os_ + (h_ + 1) * 64],
                        ont[:, os_ + h_ * 64: os_ + (h_ + 1) * 64],
                        msc[:, 4 + h_:5 + h_], None, ALU.mult),
                       wait=([("cS", cs_c)] if h == 0 else []), inc="cV")
                cv_ont = N_("cV")
                K[("odma", j)] = sc.dma("y", lambda y, j_=j, os_=oslot: y.dma_start(
                    out=o_dram[j_ * 128:(j_ + 1) * 128, :], in_=ont[:, os_:os_ + 256]),
                    wait=[("cV", cv_ont)])
        cv_dn = N_("cV")
        tk_dn = toks([K[("odma", j)] for j in range(NCH)])
        ct_dn = N_("cT")

        if PH <= 6:
            return _early()

        for i in range(SEQ // 128):
            sl = (i % 2) * 256
            w = list(tk_dn)
            if i >= 2:
                w.append(("cT", K.get(("wot", i - 2), 0)))
            tko = sc.dma("y", lambda y, i_=i, s_=sl: y.dma_start(
                out=onr[:, s_:s_ + 256], in_=o_dram[i_ * 128:(i_ + 1) * 128, :]), wait=w)
            for kk in range(2):
                wt = []
                if kk == 0:
                    wt = [tko, ("cV", K.get(("onfev", i - 1), cv_dn))]
                E_("t", lambda t, k_=kk, s_=sl: t.transpose(
                    pTb[:, k_ * 128:(k_ + 1) * 128], onr[:, s_ + k_ * 128: s_ + (k_ + 1) * 128],
                    c_idb[:, :]),
                   wait=wt, inc="cT")
            ct_onf = N_("cT")
            E_("v", lambda v, s_=sl: v.tensor_copy(onf[:, s_:s_ + 256], pTb[:, 0:256]),
               wait=[("cT", ct_onf)], inc="cV")
            cv_onf = N_("cV")
            K[("onfev", i)] = cv_onf
            for half in range(2):
                pd_ = pC if half == 0 else pD
                for kk in range(2):
                    wt = []
                    if kk == 0:
                        wt = [("cV", cv_onf), ("cV", K.get(("wopev", i - 1, half), 0))]
                    E_("t", lambda t, k_=kk, h_=half, s_=sl, po=pd_: t.matmul(
                        po[:, 0:512], onf[:, s_ + k_ * 128: s_ + (k_ + 1) * 128],
                        wo_s[:, k_ * 1024 + h_ * 512: k_ * 1024 + h_ * 512 + 512],
                        start=(k_ == 0), stop=(k_ == 1)),
                       wait=wt, inc="cT" if kk == 1 else None)
                ct_wo = N_("cT")
                K[("wot", i)] = ct_wo
                wwop = [("cT", ct_wo)]
                if i >= 2:
                    wwop.append(K[("rsin", i - 2)])
                E_("v", lambda v, h_=half, i_=i, po=pd_: v.tensor_copy(
                    wop[:, (i_ % 2) * 1024 + h_ * 512: (i_ % 2) * 1024 + (h_ + 1) * 512],
                    po[:, 0:512]),
                   wait=wwop, inc="cV")
                K[("wopev", i, half)] = N_("cV")
            cv_wop = N_("cV")
            K[("rsin", i)] = sc.dma("y", lambda y, i_=i: y.dma_start(
                out=rs_in[i_ * 128:(i_ + 1) * 128, :],
                in_=wop[:, (i_ % 2) * 1024:(i_ % 2) * 1024 + 1024]),
                wait=[("cV", cv_wop)])
        tk_rsin = toks([K[("rsin", i)] for i in range(SEQ // 128)])
        E_("g", lambda g: g.collective_compute(
            "ReduceScatter", ALU.add, replica_groups=GROUPS,
            ins=[rs_in[:, :]], outs=[rs_out[:, :]]),
           wait=tk_rsin, inc="dC")
        dc_rs = N_("dC")

        if PH <= 7:
            return _early()

        tk_out = []
        for i in range(NTT):
            sl = (i % 2) * 1024
            w = [("dC", dc_rs)]
            if i >= 2:
                w.append(("cV", K.get(("fin", i - 2), 0)))
            tkr = sc.dma("y", lambda y, i_=i, s_=sl: y.dma_start(
                out=rsb[:, s_:s_ + 1024], in_=rs_out[i_ * 128:(i_ + 1) * 128, :]), wait=w)
            E_("v", lambda v, i_=i, s_=sl: v.tensor_tensor(
                acc[:, i_ * 1024:(i_ + 1) * 1024], acc[:, i_ * 1024:(i_ + 1) * 1024],
                rsb[:, s_:s_ + 1024], ALU.add),
               wait=[tkr, ("cV", cv_moe)], inc="cV")
            cvf = N_("cV")
            K[("fin", i)] = cvf
            tk_out.append(sc.dma("g", lambda g, i_=i: g.dma_start(
                out=out[i_ * 128:(i_ + 1) * 128, :], in_=acc[:, i_ * 1024:(i_ + 1) * 1024]),
                wait=[("cV", cvf)]))
        E_("g", lambda g: None, wait=toks(tk_out), inc=None)

        _attach()
    return nc


def kernel(**inputs):
    x = np.ascontiguousarray(np.asarray(inputs["L0_attn_x"], np.float32).reshape(B * SEQ, D))
    Wq = np.asarray(inputs["L0_attn_Wq"], np.float32)
    Wk = np.asarray(inputs["L0_attn_Wk"], np.float32)
    Wv = np.asarray(inputs["L0_attn_Wv"], np.float32)
    Wo = np.asarray(inputs["L0_attn_Wo"], np.float32)
    Wbet = np.asarray(inputs["L0_attn_Wbet"], np.float32)
    Wf = np.asarray(inputs["L0_attn_Wf"], np.float32)
    norm_w = np.asarray(inputs["L0_attn_norm_w"], np.float32)
    gwT = np.ascontiguousarray(np.asarray(inputs["L0_ffn_gate_weight"], np.float32).T)
    gate_b = np.asarray(inputs["L0_ffn_e_score_correction_bias"], np.float32)
    gpw = np.ascontiguousarray(np.asarray(inputs["L0_ffn_gate_proj"], np.float32).transpose(0, 2, 1))
    upw = np.ascontiguousarray(np.asarray(inputs["L0_ffn_up_proj"], np.float32).transpose(0, 2, 1))
    dpw = np.ascontiguousarray(np.asarray(inputs["L0_ffn_down_proj"], np.float32).transpose(0, 2, 1))

    nc = build_program()
    in_maps = []
    for c in range(NCORE):
        hg = c % 4
        cols = slice(hg * HCOL, (hg + 1) * HCOL)
        hcols = slice(hg * HPC, (hg + 1) * HPC)
        in_maps.append({
            "x_in": np.ascontiguousarray(x[c * TPC:(c + 1) * TPC]),
            "wq": np.ascontiguousarray(Wq[:, cols]),
            "wk": np.ascontiguousarray(Wk[:, cols]),
            "wv": np.ascontiguousarray(Wv[:, cols]),
            "wbf": np.ascontiguousarray(np.concatenate([Wf[:, hcols], Wbet[:, hcols]], axis=1)),
            "wo_sl": np.ascontiguousarray(Wo[cols, :]),
            "nw_sl": np.ascontiguousarray(norm_w[cols][None, :]),
            "gw": gwT,
            "gbias": np.ascontiguousarray(gate_b[None, :]),
            "gp": gpw, "up": upw, "dp": dpw,
        })
    res = run_bass_kernel_spmd(nc, in_maps, core_ids=list(range(NCORE)))
    parts = [np.asarray(res.results[c]["out"], np.float32) for c in range(NCORE)]
    return np.concatenate(parts, axis=0).reshape(B, SEQ, D)


# version 03 (final)
# speedup vs baseline: 1.0046x; 1.0046x over previous
import numpy as np
import ml_dtypes
import concourse.bass as bass
import concourse.mybir as mybir
from concourse.bass_utils import run_bass_kernel_spmd

F32 = mybir.dt.float32
BF16 = mybir.dt.bfloat16
AF = mybir.ActivationFunctionType
ALU = mybir.AluOpType
AXX = mybir.AxisListType.X
BF16NP = ml_dtypes.bfloat16

B, SEQ, D, H, HD = 2, 8192, 1024, 16, 64
NCORE = 8
HPC, HCOL = 4, 256
TPC = (B * SEQ) // NCORE
C = 128
E, EI = 8, 512
NGROUP = 4
GROUPS = [[0, 1, 2, 3], [4, 5, 6, 7]]
NEG = -30000.0
LOG2 = float(np.log(2.0))
DBG = 0
RACE = 0
PH = 8


class Sched:
    ENGS = ("t", "v", "s", "g", "y")
    NDS, NDG = 8, 4

    def __init__(self):
        self.streams = {e: [] for e in self.ENGS}
        self.cnt = {}
        self.dma_idx = {"y": 0, "g": 0}

    def dma(self, eng, fn, wait=()):
        i = self.dma_idx[eng]
        self.dma_idx[eng] = i + 1
        sem = ("dS%d" % (i % self.NDS)) if eng == "y" else ("dG%d" % (i % self.NDG))
        w = list(wait)
        prev = self.cnt.get(sem, 0)
        if prev > 0:
            w.append((sem, prev))
        self.emit(eng, fn, wait=w, inc=sem, inc_by=16)
        return (sem, self.cnt[sem])

    def n(self, sem):
        return self.cnt.get(sem, 0)

    def emit(self, eng, fn, wait=(), inc=None, inc_by=1):
        w = [(s, v) for (s, v) in wait if v > 0]
        if inc is not None:
            self.cnt[inc] = self.cnt.get(inc, 0) + inc_by
        self.streams[eng].append((fn, w, inc, inc_by))
        return self.cnt.get(inc, 0)

    def replay(self, eng, h, sems):
        for fn, waits, inc, inc_by in self.streams[eng]:
            for s, v in waits:
                h.wait_ge(sems[s], v)
            ins = fn(h)
            if inc is not None:
                if ins is None:
                    ins = h.nop()
                ins.then_inc(sems[inc], inc_by)


class Reg:

    def __init__(self, t, r0, c0, nr, ncl):
        self.t, self.r0, self.c0, self.nr, self.ncl = t, r0, c0, nr, ncl

    def __getitem__(self, idx):
        rs, cs = idx
        a = self.r0 + (rs.start or 0)
        b = self.r0 + (rs.stop if rs.stop is not None else self.nr)
        c = self.c0 + (cs.start or 0)
        d = self.c0 + (cs.stop if cs.stop is not None else self.ncl)
        return self.t[a:b, c:d]


def build_program():
    NCH = SEQ // C
    NTT = TPC // 128
    NTB = NTT // 4
    nc = bass.Bass("TRN2", target_bir_lowering=False, detect_race_conditions=bool(RACE))

    x_in = nc.dram_tensor("x_in", [TPC, D], F32, kind="ExternalInput").ap()
    wq = nc.dram_tensor("wq", [D, HCOL], F32, kind="ExternalInput").ap()
    wk = nc.dram_tensor("wk", [D, HCOL], F32, kind="ExternalInput").ap()
    wv = nc.dram_tensor("wv", [D, HCOL], F32, kind="ExternalInput").ap()
    wbf = nc.dram_tensor("wbf", [D, 2 * HPC], F32, kind="ExternalInput").ap()
    wo_sl = nc.dram_tensor("wo_sl", [HCOL, D], F32, kind="ExternalInput").ap()
    nw_sl = nc.dram_tensor("nw_sl", [1, HCOL], F32, kind="ExternalInput").ap()
    gw = nc.dram_tensor("gw", [D, E], F32, kind="ExternalInput").ap()
    gbias = nc.dram_tensor("gbias", [1, E], F32, kind="ExternalInput").ap()
    gp = nc.dram_tensor("gp", [E, D, EI], F32, kind="ExternalInput").ap()
    up = nc.dram_tensor("up", [E, D, EI], F32, kind="ExternalInput").ap()
    dp = nc.dram_tensor("dp", [E, EI, D], F32, kind="ExternalInput").ap()
    out = nc.dram_tensor("out", [TPC, D], F32, kind="ExternalOutput").ap()

    ag_in = nc.dram_tensor("ag_in", [D, TPC], BF16).ap()
    ag_out = nc.dram_tensor("ag_out", [4 * D, TPC], BF16).ap()
    rs_in = nc.dram_tensor("rs_in", [SEQ, D], BF16).ap()
    rs_out = nc.dram_tensor("rs_out", [TPC, D], BF16).ap()
    o_dram = nc.dram_tensor("o_dram", [SEQ, HCOL], BF16).ap()
    if DBG & 32:
        dbgX = nc.dram_tensor("dbgX", [128, 8 * TPC], BF16, kind="ExternalOutput").ap()
        dbgY = nc.dram_tensor("dbgY", [128, 16384], BF16, kind="ExternalOutput").ap()
    if DBG & 64:
        dbgG = nc.dram_tensor("dbgG", [128, 8 * (TPC // 128) + 1024], F32,
                              kind="ExternalOutput").ap()
    if DBG:
        dbgA = nc.dram_tensor("dbgA", [128, 12288], BF16).ap()
        dbgB = nc.dram_tensor("dbgB", [128, 2048], BF16).ap()
        dbgC = nc.dram_tensor("dbgC", [128, (TPC // 128) * 1024], F32).ap()
        dbgD = nc.dram_tensor("dbgD", [E * 128, (TPC // 128) * 1024], F32).ap()

    ident_b = nc.inline_tensor(np.eye(128, dtype=np.float32).astype(BF16NP), name="ident_b").ap()
    ident_f = nc.inline_tensor(np.eye(128, dtype=np.float32), name="ident_f").ap()
    ii = np.arange(128)
    mask1 = nc.inline_tensor(np.where(ii[None, :] <= ii[:, None], NEG, 0.0).astype(np.float32),
                             name="mask1").ap()
    mask2 = nc.inline_tensor(np.where(ii[None, :] >= ii[:, None], NEG, 0.0).astype(np.float32),
                             name="mask2").ap()
    incl01 = nc.inline_tensor((ii[:, None] <= ii[None, :]).astype(np.float32), name="incl01").ap()
    ones_row = nc.inline_tensor(np.ones((1, 128), np.float32), name="ones_row").ap()
    ones_c64 = nc.inline_tensor(np.ones((64, 1), np.float32), name="ones_c64").ap()
    epsv = nc.inline_tensor(np.tile(np.array([[1e-5, 1e-12, 1e-6, 0.0, 1e-20, 0.0, 0.0, 0.0]],
                                              np.float32), (128, 1)), name="epsv").ap()

    sc = Sched()
    E_, N_ = sc.emit, sc.n
    K = sc.cnt

    def toks(lst):
        d = {}
        for (sm, v) in lst:
            d[sm] = max(d.get(sm, 0), v)
        return list(d.items())

    import contextlib
    with contextlib.ExitStack() as _stk:
        _e = _stk.enter_context
        acc = _e(nc.sbuf_tensor("acc", [128, NTT * 1024], F32))
        xnh_f = _e(nc.sbuf_tensor("xnh_f", [128, 8 * TPC], BF16))
        ar16 = _e(nc.sbuf_tensor("ar16", [128, 30720], BF16))
        ar32 = _e(nc.sbuf_tensor("ar32", [128, 4096], F32))
        c_idb = _e(nc.sbuf_tensor("c_idb", [128, 128], BF16))
        c_idf = _e(nc.sbuf_tensor("c_idf", [128, 128], F32))
        c_m1 = _e(nc.sbuf_tensor("c_m1", [128, 128], F32))
        c_m2 = _e(nc.sbuf_tensor("c_m2", [128, 128], F32))
        c_incl = _e(nc.sbuf_tensor("c_incl", [128, 128], F32))
        c_or = _e(nc.sbuf_tensor("c_or", [1, 128], F32))
        c_oc = _e(nc.sbuf_tensor("c_oc", [64, 1], F32))
        c_eps = _e(nc.sbuf_tensor("c_eps", [128, 8], F32))
        wq_s = _e(nc.sbuf_tensor("wq_s", [128, 8 * HCOL], BF16))
        wkv_s = _e(nc.sbuf_tensor("wkv_s", [128, 8 * 512], BF16))
        wbf_s = _e(nc.sbuf_tensor("wbf_s", [128, 8 * 8], BF16))
        wo_s = _e(nc.sbuf_tensor("wo_s", [128, 2 * D], BF16))
        pass
        nwc = _e(nc.sbuf_tensor("nwc", [128, 2], F32))
        gwt_s = _e(nc.sbuf_tensor("gwt_s", [128, 8 * E], BF16))
        gwt_lo = _e(nc.sbuf_tensor("gwt_lo", [128, 8 * E], BF16))
        gwt_f = _e(nc.sbuf_tensor("gwt_f", [128, 8 * E], F32))
        gb_row = _e(nc.sbuf_tensor("gb_row", [1, E], F32))
        xtile = _e(nc.sbuf_tensor("xtile", [128, 1024], F32))
        pass
        lnw = _e(nc.sbuf_tensor("lnw", [128, 16], F32))
        cw_all = _e(nc.sbuf_tensor("cw_all", [128, 8 * NTT], F32))
        lblgT = _e(nc.sbuf_tensor("lblgT", [128, 8 * NCH], F32))
        d1a = _e(nc.sbuf_tensor("d1a", [128, 4 * NCH], F32))
        sbar = _e(nc.sbuf_tensor("sbar", [128, 4 * NCH], F32))
        shk = _e(nc.sbuf_tensor("shk", [128, 4 * NCH], F32))
        gga = _e(nc.sbuf_tensor("gga", [128, 4 * NCH], F32))
        gg2 = _e(nc.sbuf_tensor("gg2", [128, 4 * NCH], F32))
        dkb = _e(nc.sbuf_tensor("dkb", [128, 4 * NCH], F32))
        pass
        pass
        pass
        pass
        pass
        pass
        pass
        zer8 = _e(nc.sbuf_tensor("zer8", [8, 128], F32))
        pass
        pass
        esk = _e(nc.sbuf_tensor("esk", [128, 512], F32))
        pass
        knc = _e(nc.sbuf_tensor("knc", [128, 16], F32))
        pass
        pass
        pass
        pass
        pass
        pass
        pass
        pass
        pass
        pass
        pass
        pass
        pass
        pass
        pass
        pass
        pass
        pass
        pass
        pass
        pass
        pass
        pass
        msc = _e(nc.sbuf_tensor("msc", [128, 16], F32))
        pass
        pass
        pass
        pass
        pass
        pA = _e(nc.psum_tensor("pA", [128, 512], F32))
        pB = _e(nc.psum_tensor("pB", [128, 512], F32))
        pC = _e(nc.psum_tensor("pC", [128, 512], F32))
        pD = _e(nc.psum_tensor("pD", [128, 512], F32))
        pE = _e(nc.psum_tensor("pE", [128, 512], F32))
        pF = _e(nc.psum_tensor("pF", [128, 512], F32))
        pTb = _e(nc.psum_tensor("pTb", [128, 1024], BF16))
        pTb2 = _e(nc.psum_tensor("pTb2", [128, 1024], BF16))
        s_dS = [_e(nc.semaphore("dS%d" % i)) for i in range(Sched.NDS)]
        s_dG = [_e(nc.semaphore("dG%d" % i)) for i in range(Sched.NDG)]
        s_dC = _e(nc.semaphore("dC"))
        s_cT = _e(nc.semaphore("cT"))
        s_cV = _e(nc.semaphore("cV"))
        s_cS = _e(nc.semaphore("cS"))
        block = _e(nc.Block())
        wof = Reg(ar32, 0, 0, 128, 2048)
        zb8 = Reg(ar32, 0, 840, 8, 512)
        zb8b = Reg(ar32, 0, 1352, 8, 512)
        kmf = Reg(ar32, 0, 2048, 128, 256)
        ksq = Reg(ar32, 0, 2304, 128, 256)
        osq = Reg(ar32, 0, 2560, 64, 512)
        mrow = Reg(ar32, 0, 3072, 1, 4 * NCH)
        lgcrow = Reg(ar32, 0, 3328, 1, 4 * NCH)
        mnrow = Reg(ar32, 0, 3584, 1, 4 * NCH)
        zzrow = Reg(ar32, 0, 3840, 1, 4 * NCH)
        kvt = Reg(ar16, 0, 8192, 128, 2048)
        kck = Reg(ar16, 0, 10240, 128, 1024)
        knm = Reg(ar16, 0, 11264, 128, 1024)
        qft = Reg(ar16, 0, 12288, 128, 1024)
        kft = Reg(ar16, 0, 13312, 128, 1024)
        offb = Reg(ar16, 0, 14336, 64, 512)
        onr = Reg(ar16, 0, 14848, 128, 512)
        onf = Reg(ar16, 0, 15360, 128, 512)
        wop = Reg(ar16, 0, 15872, 128, 2048)
        rsb = Reg(ar16, 0, 17920, 128, 2048)
        xn32 = Reg(ar32, 0, 0, 128, 1024)
        swk = Reg(ar32, 0, 0, 128, 4 * NCH)
        cbt = Reg(ar32, 0, 256, 128, 128)
        ssf = Reg(ar32, 0, 384, 128, 128)
        xng = Reg(ar16, 0, 22336, 128, 2048)
        xx = Reg(ar16, 0, 19968, 128, 128)
        xxt = Reg(ar16, 0, 20096, 128, 128)
        a2 = Reg(ar16, 0, 20224, 128, 128)
        a2t = Reg(ar16, 0, 20352, 128, 128)
        a4 = Reg(ar16, 0, 20480, 128, 128)
        a4t = Reg(ar16, 0, 20608, 128, 128)
        a8t = Reg(ar16, 0, 20736, 128, 128)
        zz1 = Reg(ar16, 0, 20864, 128, 128)
        zz2 = Reg(ar16, 0, 20992, 128, 128)
        mt = Reg(ar16, 0, 21120, 128, 128)
        wfb = Reg(ar16, 0, 21248, 128, 128)
        uu = Reg(ar16, 0, 21376, 128, 64)
        ea_t = Reg(ar16, 0, 21440, 128, 256)
        ssb = Reg(ar16, 0, 21696, 128, 128)
        ont = Reg(ar16, 0, 21824, 128, 512)
        sems = {"dC": s_dC, "cT": s_cT, "cV": s_cV, "cS": s_cS}
        for i in range(Sched.NDS):
            sems["dS%d" % i] = s_dS[i]
        for i in range(Sched.NDG):
            sems["dG%d" % i] = s_dG[i]

        def _attach():
            @block.sync
            def _(y):
                sc.replay("y", y, sems)

            @block.gpsimd
            def _(g):
                sc.replay("g", g, sems)

            @block.vector
            def _(v):
                sc.replay("v", v, sems)

            @block.scalar
            def _(s):
                sc.replay("s", s, sems)

            @block.tensor
            def _(t):
                sc.replay("t", t, sems)

        def _early():
            wv, wt, ws = sc.n("cV"), sc.n("cT"), sc.n("cS")
            tk = []
            for i in range(NTT):
                tk.append(sc.dma("g", lambda g, i_=i: g.dma_start(
                    out=out[i_ * 128:(i_ + 1) * 128, :],
                    in_=acc[:, i_ * 1024:(i_ + 1) * 1024]),
                    wait=[("cV", wv), ("cT", wt), ("cS", ws)]))
            if DBG & 64:
                tk.append(sc.dma("y", lambda y: y.dma_start(
                    out=dbgG[:, 0:8 * NTT], in_=cw_all[:, :]),
                    wait=[("cV", wv), ("cT", wt), ("cS", ws)]))
                tk.append(sc.dma("y", lambda y: y.dma_start(
                    out=dbgG[:, 8 * NTT:8 * NTT + 1024], in_=ar32[:, 0:1024]),
                    wait=[("cV", wv), ("cT", wt), ("cS", ws)]))
            if DBG & 32:
                tk.append(sc.dma("y", lambda y: y.dma_start(
                    out=dbgX[:, :], in_=xnh_f[:, 0:8 * TPC]),
                    wait=[("cV", wv), ("cT", wt), ("cS", ws)]))
                tk.append(sc.dma("y", lambda y: y.dma_start(
                    out=dbgY[:, :], in_=ar16[:, 0:16384]),
                    wait=[("cV", wv), ("cT", wt), ("cS", ws)]))
            E_("g", lambda g: None, wait=toks(tk))
            _attach()
            return nc

        E_("v", lambda v: v.memset(lnw[:, :], 0.0), inc="cV")
        for _ in range(6):
            E_("v", lambda v: v.tensor_scalar(lnw[:, 0:1], lnw[:, 0:1], 1.0, None, ALU.add),
               inc="cV")

        tk_const = []
        for (dst, src) in [(c_idb, ident_b), (c_idf, ident_f), (c_m1, mask1),
                           (c_m2, mask2), (c_incl, incl01), (c_or, ones_row),
                           (c_oc, ones_c64), (c_eps, epsv), (gb_row, gbias)]:
            tk_const.append(sc.dma("y", lambda y, d=dst, s=src: y.dma_start(
                out=d[:, :], in_=s[:, :])))
        tk_const = toks(tk_const)
        tk_attnw = toks([
            sc.dma("g", lambda g: g.dma_start(
                out=wq_s[:, :].rearrange("p (dc j) -> p dc j", dc=8),
                in_=wq.rearrange("(dc p) j -> p dc j", p=128))),
            sc.dma("g", lambda g: g.dma_start(
                out=wkv_s[:, :].rearrange("p (dc x j) -> p dc x j", dc=8, x=2)[:, :, 0, :],
                in_=wk.rearrange("(dc p) j -> p dc j", p=128))),
            sc.dma("g", lambda g: g.dma_start(
                out=wkv_s[:, :].rearrange("p (dc x j) -> p dc x j", dc=8, x=2)[:, :, 1, :],
                in_=wv.rearrange("(dc p) j -> p dc j", p=128))),
            sc.dma("g", lambda g: g.dma_start(
                out=wbf_s[:, :].rearrange("p (dc j) -> p dc j", dc=8),
                in_=wbf.rearrange("(dc p) j -> p dc j", p=128))),
            sc.dma("g", lambda g: g.dma_start(
                out=gwt_s[:, :].rearrange("p (dc e) -> p dc e", dc=8),
                in_=gw.rearrange("(dc p) e -> p dc e", p=128)))])
        tk_w0 = toks([
            sc.dma("y", lambda y: y.dma_start(
                out=gwt_f[:, :].rearrange("p (dc e) -> p dc e", dc=8),
                in_=gw.rearrange("(dc p) e -> p dc e", p=128))),
            sc.dma("y", lambda y: y.dma_start(
                out=wof[:, :].rearrange("p (c j) -> p c j", c=2),
                in_=wo_sl.rearrange("(c p) j -> p c j", p=128))),
            sc.dma("y", lambda y: y.dma_start(
                out=nwc[:, :], in_=nw_sl.rearrange("o (c p) -> p (o c)", p=128),
                allow_slow_non_contiguous=True))])
        E_("v", lambda v: v.tensor_tensor(gwt_lo[:, :], gwt_f[:, :], gwt_s[:, :], ALU.subtract),
           wait=tk_w0 + tk_attnw, inc="cV")
        for cch in range(2):
            E_("v", lambda v, cc=cch: v.tensor_scalar(
                wo_s[:, cc * 1024:(cc + 1) * 1024], wof[:, cc * 1024:(cc + 1) * 1024],
                nwc[:, cc:cc + 1], None, ALU.mult), inc="cV")
        cv_w0s = N_("cV")
        E_("v", lambda v: v.memset(zer8[:, :], 0.0), inc="cV")


        if PH <= 0:
            return _early()

        XH, XL, XNL = 24576, 26624, 0
        xfree = {}
        for i in range(NTT):
            tb = 0
            w = [("cV", xfree[i - 1])] if i >= 1 else []
            tkx = sc.dma("y", lambda y, i_=i, t=tb: y.dma_start(
                out=xtile[:, t:t + 1024], in_=x_in[i_ * 128:(i_ + 1) * 128, :]), wait=w)
            w = [tkx]
            E_("v", lambda v, t=tb: v.reduce_sum(lnw[:, 0:1], xtile[:, t:t + 1024], AXX),
               wait=w, inc="cV")
            cv_red1 = N_("cV")
            E_("s", lambda s, t=tb: s.activation(xn32[:, :], xtile[:, t:t + 1024], AF.Square),
               wait=w + ([("cV", K.get(("hilo", i - 1), 0))] if i >= 1 else [("cV", cv_w0s)]),
               inc="cS")
            cs_sq = N_("cS")
            E_("v", lambda v: v.reduce_sum(lnw[:, 1:2], xn32[:, :], AXX),
               wait=[("cS", cs_sq)], inc="cV")
            cv_red2 = N_("cV")
            E_("v", lambda v: v.tensor_scalar(lnw[:, 3:4], lnw[:, 1:2], 1.0 / D, None, ALU.mult),
               wait=[("cV", cv_red2)], inc="cV")
            E_("v", lambda v: v.tensor_scalar(lnw[:, 2:3], lnw[:, 0:1], 1.0 / D, None, ALU.mult),
               wait=[("cV", cv_red1)], inc="cV")
            cv_mean = N_("cV")
            E_("s", lambda s: s.activation(lnw[:, 4:5], lnw[:, 2:3], AF.Square),
               wait=[("cV", cv_mean)], inc="cS")
            cs_m2 = N_("cS")
            E_("v", lambda v: v.tensor_tensor(lnw[:, 5:6], lnw[:, 3:4], lnw[:, 4:5], ALU.subtract),
               wait=[("cS", cs_m2)], inc="cV")
            cv_var = N_("cV")
            E_("s", lambda s: s.activation(lnw[:, 6:7], lnw[:, 5:6], AF.Ln, bias=c_eps[:, 0:1], scale=1.0),
               wait=[("cV", cv_var)], inc="cS")
            E_("s", lambda s: s.activation(lnw[:, 7:8], lnw[:, 6:7], AF.Exp, bias=0.0, scale=-0.5),
               inc="cS")
            cs_rstd = N_("cS")
            E_("v", lambda v, t=tb: v.tensor_scalar(
                xn32[:, :], xtile[:, t:t + 1024], lnw[:, 2:3], lnw[:, 7:8],
                ALU.subtract, ALU.mult), wait=[("cS", cs_rstd)], inc="cV")
            E_("v", lambda v, t=tb, i_=i: v.tensor_copy(
                acc[:, i_ * 1024:(i_ + 1) * 1024], xtile[:, t:t + 1024]), inc="cV")
            xfree[i] = N_("cV")
            ho, lo = XH + tb, XL + tb
            E_("v", lambda v, h_=ho: v.tensor_copy(ar16[:, h_:h_ + 1024], xn32[:, :]),
               wait=[("cT", K.get(("lntp", i - 2), 0))], inc="cV")
            E_("v", lambda v, h_=ho, l_=lo: v.tensor_tensor(
                ar16[:, l_:l_ + 1024], xn32[:, :], ar16[:, h_:h_ + 1024], ALU.subtract), inc="cV")
            K[("hilo", i)] = N_("cV")
            for (reg, pt) in [("h", None), ("l", None)]:
                src0 = ho if reg == "h" else lo
                for dc in range(8):
                    b = dc % 2
                    pt_ = pTb if b == 0 else pTb2
                    wt = [("cV", K.get(("ptbfree", b), 0))]
                    if dc == 0 and reg == "h":
                        wt += [("cV", K[("hilo", i)])] + tk_const
                    E_("t", lambda t, s0=src0, d=dc, p_=pt_: t.transpose(
                        p_[:, 0:128],
                        ar16[:, s0 + d * 128: s0 + d * 128 + 128], c_idb[:, :]),
                       wait=wt, inc="cT")
                    ct1 = N_("cT")
                    if reg == "h":
                        E_("v", lambda v, d=dc, i_=i, p_=pt_: v.tensor_copy(
                            xnh_f[:, d * TPC + i_ * 128: d * TPC + i_ * 128 + 128],
                            p_[:, 0:128]),
                           wait=[("cT", ct1)], inc="cV")
                    else:
                        E_("v", lambda v, d=dc, i_=i, p_=pt_: v.tensor_copy(
                            ar16[:, XNL + d * 2048 + i_ * 128: XNL + d * 2048 + i_ * 128 + 128],
                            p_[:, 0:128]),
                           wait=[("cT", ct1)], inc="cV")
                    K[("ptbfree", b)] = N_("cV")
                    K[("tev", i, dc, reg)] = K[("ptbfree", b)]
            K[("lntp", i)] = N_("cT")
        cv_ln = N_("cV")

        if PH <= 1:
            return _early()

        tk_agin = sc.dma("y", lambda y: y.dma_start(
            out=ag_in[:, :].rearrange("(dc p) t -> p dc t", p=128),
            in_=xnh_f[:, :].rearrange("p (dc t) -> p dc t", dc=8)),
            wait=[("cV", cv_ln)])
        E_("g", lambda g: g.collective_compute(
            "AllGather", ALU.bypass, replica_groups=GROUPS,
            ins=[ag_in[:, :]], outs=[ag_out[:, :]]),
           wait=[tk_agin], inc="dC")
        dc_ag = N_("dC")

        if PH <= 2:
            return _early()

        def VJ():
            E_("v", lambda v: v.tensor_scalar(zer8[0:8, 0:8], zer8[0:8, 0:8],
                                              1.0, None, ALU.mult), inc=None)

        SCT = 0
        nW = 8 * NTT
        for tb in range(NTB):
            toff = tb * 512
            first = True
            for (lw, rl) in [(gwt_s, "h"), (gwt_s, "l"), (gwt_lo, "h")]:
                for dc in range(8):
                    if rl == "h":
                        rap = (lambda d, to: (lambda: xnh_f[:, d * TPC + to: d * TPC + to + 512]))(dc, toff)
                    else:
                        rap = (lambda d, to: (lambda: ar16[:, XNL + d * 2048 + to: XNL + d * 2048 + to + 512]))(dc, toff)
                    last = (rl == "h" and lw is gwt_lo and dc == 7)
                    wt = []
                    if first:
                        wt = [("cV", cv_ln), ("cV", K.get(("glog", tb - 1), 0))]
                    E_("t", lambda t, lw_=lw, d=dc, ra=rap, f_=first, l_=last: t.matmul(
                        pA[0:E, 0:512], lw_[:, d * E:(d + 1) * E], ra(),
                        start=f_, stop=l_),
                       wait=wt, inc="cT" if last else None)
                    first = False
            ct_lg = N_("cT")
            E_("v", lambda v: v.tensor_copy(zb8[0:E, 0:512], pA[0:E, 0:512]),
               wait=[("cT", ct_lg)], inc="cV")
            K[("glog", tb)] = N_("cV")
            E_("s", lambda s: s.activation(zb8b[0:E, 0:512], zb8[0:E, 0:512], AF.Sigmoid),
               wait=[("cV", K[("glog", tb)])], inc="cS")
            cs_sig = N_("cS")
            for q4 in range(4):
                E_("t", lambda t, q=q4: t.transpose(
                    pB[:, q * 8:q * 8 + 8], zb8b[0:E, q * 128:(q + 1) * 128], c_idf[0:E, 0:E]),
                   wait=([("cS", cs_sig), ("cV", K.get(("gsct", tb - 1), 0))] if q4 == 0 else []),
                   inc="cT")
            ct_tp = N_("cT")
            E_("v", lambda v, tb_=tb: v.tensor_copy(
                ar32[:, SCT + tb_ * 32: SCT + tb_ * 32 + 32], pB[:, 0:32]),
               wait=[("cT", ct_tp)], inc="cV")
            K[("gsct", tb)] = N_("cV")
        cv_sct = N_("cV")
        E_("t", lambda t: t.matmul(pC[:, 0:E], c_or[0:1, 0:128], gb_row[0:1, 0:E],
                                   start=True, stop=True),
           wait=tk_const, inc="cT")
        ct_bb = N_("cT")
        BB = nW
        SFC = BB + 8
        GS = SFC + nW
        GR = GS + nW // 2
        TMP = GR + nW // 2
        TMP8 = TMP + nW // 2
        ER = TMP8 + nW
        ETMP = ER + nW
        E_("v", lambda v: v.tensor_copy(ar32[:, BB:BB + 8], pC[:, 0:E]),
           wait=[("cT", ct_bb)], inc="cV")
        VJ()
        E_("v", lambda v: v.tensor_tensor(
            ar32[:, SFC:SFC + nW], ar32[:, SCT:SCT + nW],
            ar32[:, BB:BB + 8].rearrange("p (o e) -> p o e", o=1).broadcast_to([128, NTT, 8]),
            ALU.add), wait=[("cV", cv_sct)], inc="cV")
        VJ()
        E_("v", lambda v: v.tensor_tensor(
            ar32[:, GS:GS + nW // 2],
            ar32[:, SFC:SFC + nW].rearrange("p (x two) -> p x two", two=2)[:, :, 0],
            ar32[:, SFC:SFC + nW].rearrange("p (x two) -> p x two", two=2)[:, :, 1],
            ALU.add), inc="cV")
        E_("v", lambda v: v.memset(ar32[:, GR:GR + nW // 2], 0.0), inc="cV")
        for gsel in range(NGROUP):
            E_("v", lambda v, g_=gsel: v.tensor_tensor(
                ar32[:, TMP:TMP + nW // 2],
                ar32[:, GS:GS + nW // 2].rearrange("p (i g) -> p i g", g=4)[:, :, g_:g_ + 1]
                    .broadcast_to([128, NTT, 4]),
                ar32[:, GS:GS + nW // 2].rearrange("p (i g) -> p i g", g=4),
                ALU.is_gt), inc="cV")
            VJ()
            E_("v", lambda v: v.tensor_tensor(
                ar32[:, GR:GR + nW // 2], ar32[:, GR:GR + nW // 2],
                ar32[:, TMP:TMP + nW // 2], ALU.add), inc="cV")
            if gsel < NGROUP - 1:
                nrest = NGROUP - 1 - gsel
                E_("v", lambda v, g_=gsel, nr=nrest: v.tensor_tensor(
                    ar32[:, TMP:TMP + nW // 2].rearrange("p (i g) -> p i g", g=4)[:, :, g_ + 1:4],
                    ar32[:, GS:GS + nW // 2].rearrange("p (i g) -> p i g", g=4)[:, :, g_:g_ + 1]
                        .broadcast_to([128, NTT, nr]),
                    ar32[:, GS:GS + nW // 2].rearrange("p (i g) -> p i g", g=4)[:, :, g_ + 1:4],
                    ALU.is_equal), inc="cV")
                VJ()
                E_("v", lambda v, g_=gsel: v.tensor_tensor(
                    ar32[:, GR:GR + nW // 2].rearrange("p (i g) -> p i g", g=4)[:, :, g_ + 1:4],
                    ar32[:, GR:GR + nW // 2].rearrange("p (i g) -> p i g", g=4)[:, :, g_ + 1:4],
                    ar32[:, TMP:TMP + nW // 2].rearrange("p (i g) -> p i g", g=4)[:, :, g_ + 1:4],
                    ALU.add), inc="cV")
        VJ()
        E_("v", lambda v: v.tensor_scalar(
            ar32[:, GR:GR + nW // 2], ar32[:, GR:GR + nW // 2], 2.0, None, ALU.is_lt), inc="cV")
        VJ()
        E_("v", lambda v: v.tensor_tensor(
            ar32[:, TMP8:TMP8 + nW].rearrange("p (x two) -> p x two", two=2),
            ar32[:, SFC:SFC + nW].rearrange("p (x two) -> p x two", two=2),
            ar32[:, GR:GR + nW // 2].rearrange("p (x o) -> p x o", o=1).broadcast_to([128, nW // 2, 2]),
            ALU.mult), inc="cV")
        E_("v", lambda v: v.memset(ar32[:, ER:ER + nW], 0.0), inc="cV")
        for esel in range(E):
            E_("v", lambda v, e_=esel: v.tensor_tensor(
                ar32[:, ETMP:ETMP + nW],
                ar32[:, TMP8:TMP8 + nW].rearrange("p (i e) -> p i e", e=8)[:, :, e_:e_ + 1]
                    .broadcast_to([128, NTT, 8]),
                ar32[:, TMP8:TMP8 + nW].rearrange("p (i e) -> p i e", e=8),
                ALU.is_gt), inc="cV")
            VJ()
            E_("v", lambda v: v.tensor_tensor(
                ar32[:, ER:ER + nW], ar32[:, ER:ER + nW], ar32[:, ETMP:ETMP + nW], ALU.add), inc="cV")
            if esel < E - 1:
                nrest = E - 1 - esel
                E_("v", lambda v, e_=esel, nr=nrest: v.tensor_tensor(
                    ar32[:, ETMP:ETMP + nW].rearrange("p (i e) -> p i e", e=8)[:, :, e_ + 1:8],
                    ar32[:, TMP8:TMP8 + nW].rearrange("p (i e) -> p i e", e=8)[:, :, e_:e_ + 1]
                        .broadcast_to([128, NTT, nr]),
                    ar32[:, TMP8:TMP8 + nW].rearrange("p (i e) -> p i e", e=8)[:, :, e_ + 1:8],
                    ALU.is_equal), inc="cV")
                VJ()
                E_("v", lambda v, e_=esel: v.tensor_tensor(
                    ar32[:, ER:ER + nW].rearrange("p (i e) -> p i e", e=8)[:, :, e_ + 1:8],
                    ar32[:, ER:ER + nW].rearrange("p (i e) -> p i e", e=8)[:, :, e_ + 1:8],
                    ar32[:, ETMP:ETMP + nW].rearrange("p (i e) -> p i e", e=8)[:, :, e_ + 1:8],
                    ALU.add), inc="cV")
        VJ()
        E_("v", lambda v: v.tensor_scalar(
            ar32[:, ER:ER + nW], ar32[:, ER:ER + nW], 2.0, None, ALU.is_lt), inc="cV")
        VJ()
        WSUM, WS2, WINV, WINV2, ETMP2 = 1792, 1808, 1824, 1840, 1856
        E_("v", lambda v: v.tensor_tensor(
            ar32[:, ETMP:ETMP + nW], ar32[:, SCT:SCT + nW], ar32[:, ER:ER + nW], ALU.mult), inc="cV")
        cv_et = N_("cV")
        E_("s", lambda s: s.activation(ar32[:, ETMP2:ETMP2 + nW], ar32[:, ETMP:ETMP + nW],
                                       AF.Copy), wait=[("cV", cv_et)], inc="cS")
        cs_et2 = N_("cS")
        for i in range(NTT):
            E_("v", lambda v, i_=i: v.reduce_sum(
                ar32[:, WSUM + i_:WSUM + i_ + 1],
                ar32[:, ETMP2 + i_ * 8:ETMP2 + i_ * 8 + 8], AXX),
               wait=([("cS", cs_et2)] if i == 0 else []), inc="cV")
        cv_ws = N_("cV")
        E_("s", lambda s: s.activation(ar32[:, WS2:WS2 + NTT], ar32[:, WSUM:WSUM + NTT],
                                       AF.Ln, bias=c_eps[:, 4:5], scale=1.0),
           wait=[("cV", cv_ws)], inc="cS")
        E_("s", lambda s: s.activation(ar32[:, WINV2:WINV2 + NTT], ar32[:, WS2:WS2 + NTT],
                                       AF.Exp, bias=0.0, scale=-1.0), inc="cS")
        cs_wi = N_("cS")
        for i in range(NTT):
            E_("v", lambda v, i_=i: v.tensor_scalar(
                cw_all[:, i_ * 8:(i_ + 1) * 8], ar32[:, ETMP2 + i_ * 8:ETMP2 + i_ * 8 + 8],
                ar32[:, WINV2 + i_:WINV2 + i_ + 1], None, ALU.mult),
               wait=([("cS", cs_wi)] if i == 0 else []), inc="cV")
        cv_gate = N_("cV")

        if PH <= 3:
            return _early()

        def emit_wload(ee, w):
            wb_ = (ee % 2) * 12288
            return toks([
                sc.dma("g", lambda g, e_=ee, b_=wb_: g.dma_start(
                    out=ar16[:, b_:b_ + 4096].rearrange("p (hc i) -> p hc i", hc=8),
                    in_=gp[e_, :, :].rearrange("(hc p) i -> p hc i", p=128)), wait=w),
                sc.dma("g", lambda g, e_=ee, b_=wb_: g.dma_start(
                    out=ar16[:, b_ + 4096:b_ + 8192].rearrange("p (hc i) -> p hc i", hc=8),
                    in_=up[e_, :, :].rearrange("(hc p) i -> p hc i", p=128))),
                sc.dma("g", lambda g, e_=ee, b_=wb_: g.dma_start(
                    out=ar16[:, b_ + 8192:b_ + 12288].rearrange("p (ic j) -> p ic j", ic=4),
                    in_=dp[e_, :, :].rearrange("(ic p) j -> p ic j", p=128)))])

        tk_moew = [emit_wload(0, [("cV", cv_gate)]),
                   emit_wload(1, [("cV", cv_gate)])]
        PRD, GUE = 24576, 28672
        prodslots = {}
        tok_dD = {}
        flat = 0
        prev_pa = 0
        prev_pb = 0
        for e in range(E):
            wb = (e % 2) * 12288
            if (DBG & 4) and e == 1:
                tok_dC = sc.dma("y", lambda y: y.dma_start(
                    out=dbgC[:, :], in_=acc[:, 0:NTT * 1024]),
                    wait=[("cV", K[("moeacc", 0, NTT - 1, 0)]),
                          ("cV", K[("moeacc", 0, NTT - 1, 1)])])
            for tb in range(NTB):
                toff = tb * 512
                pb = PRD + (flat % 2) * 2048
                if (DBG & 2) and flat == 2:
                    tok_dB = sc.dma("y", lambda y: y.dma_start(
                        out=dbgB[:, :], in_=ar16[:, PRD:PRD + 2048]),
                        wait=[("cV", K[("moeprodv", 0, 3)])])
                cv_prod = 0
                for ic in range(4):
                    wmm = []
                    if tb == 0 and ic == 0:
                        wmm = tk_moew[e] + [("cV", cv_gate)]
                    wmm.append(("cS", prev_pa))
                    if ic == 0 and flat >= 2:
                        wmm.append(("cT", prodslots[flat - 2]))
                    if (DBG & 2) and flat == 2 and ic == 0:
                        wmm.append(tok_dB)
                    for dc in range(8):
                        E_("t", lambda t, h_=dc, i_=ic, wb_=wb, to=toff: t.matmul(
                            pA[:, 0:512],
                            ar16[:, wb_ + h_ * 512 + i_ * 128: wb_ + h_ * 512 + i_ * 128 + 128],
                            xnh_f[:, h_ * TPC + to: h_ * TPC + to + 512],
                            start=(h_ == 0), stop=(h_ == 7)),
                           wait=(wmm if dc == 0 else []), inc="cT" if dc == 7 else None)
                    ct_g = N_("cT")
                    for dc in range(8):
                        E_("t", lambda t, h_=dc, i_=ic, wb_=wb, to=toff: t.matmul(
                            pB[:, 0:512],
                            ar16[:, wb_ + 4096 + h_ * 512 + i_ * 128: wb_ + 4096 + h_ * 512 + i_ * 128 + 128],
                            xnh_f[:, h_ * TPC + to: h_ * TPC + to + 512],
                            start=(h_ == 0), stop=(h_ == 7)),
                           wait=([("cV", prev_pb)] if dc == 0 else []),
                           inc="cT" if dc == 7 else None)
                    ct_u = N_("cT")
                    E_("s", lambda s: s.activation(esk[:, 0:512], pA[:, 0:512], AF.Silu),
                       wait=[("cT", ct_g),
                             ("cV", K.get(("moeprodv", flat, ic - 1),
                                          K.get(("moeprodv", flat - 1, 3), 0)))], inc="cS")
                    prev_pa = N_("cS")
                    E_("v", lambda v: v.tensor_copy(ar16[:, GUE + 512:GUE + 1024], pB[:, 0:512]),
                       wait=[("cT", ct_u)], inc="cV")
                    prev_pb = N_("cV")
                    E_("v", lambda v, p_=pb, i_=ic: v.tensor_tensor(
                        ar16[:, p_ + i_ * 512: p_ + (i_ + 1) * 512],
                        esk[:, 0:512], ar16[:, GUE + 512:GUE + 1024], ALU.mult),
                       wait=[("cS", prev_pa)], inc="cV")
                    cv_prod = N_("cV")
                    K[("moeprodv", flat, ic)] = cv_prod
                for q4 in range(4):
                    gtile = tb * 4 + q4
                    for half in range(2):
                        pd_ = pC if half == 0 else pD
                        wdn = []
                        if q4 == 0 and half == 0:
                            wdn.append(("cV", cv_prod))
                        wdn.append(("cV", K.get(("moeacc", e, gtile - 1, half),
                                               K.get(("moeaccprev", e, 1), 0))))
                        if (DBG & 4) and e == 1 and gtile == 0:
                            wdn.append(tok_dC)
                        if (DBG & 24) and e >= 1 and gtile == 0 and (e - 1) in tok_dD:
                            wdn.append(tok_dD[e - 1])
                        for ic in range(4):
                            E_("t", lambda t, i_=ic, q_=q4, h_=half, wb_=wb, p_=pb, po=pd_: t.matmul(
                                po[:, 0:512],
                                ar16[:, p_ + i_ * 512 + q_ * 128: p_ + i_ * 512 + q_ * 128 + 128],
                                ar16[:, wb_ + 8192 + i_ * 1024 + h_ * 512: wb_ + 8192 + i_ * 1024 + (h_ + 1) * 512],
                                start=(i_ == 0), stop=(i_ == 3)),
                               wait=(wdn if ic == 0 else []), inc="cT" if ic == 3 else None)
                        ct_d = N_("cT")
                        E_("v", lambda v, h_=half, g_=gtile, ee=e, po=pd_: v.scalar_tensor_tensor(
                            acc[:, g_ * 1024 + h_ * 512: g_ * 1024 + (h_ + 1) * 512],
                            po[:, 0:512],
                            cw_all[:, g_ * 8 + ee: g_ * 8 + ee + 1],
                            acc[:, g_ * 1024 + h_ * 512: g_ * 1024 + (h_ + 1) * 512],
                            ALU.mult, ALU.add),
                           wait=[("cT", ct_d)], inc="cV")
                        K[("moeacc", e, gtile, half)] = N_("cV")
                prodslots[flat] = N_("cT")
                flat += 1
            K[("moeaccprev", e + 1, 0)] = K[("moeacc", e, NTT - 1, 0)]
            K[("moeaccprev", e + 1, 1)] = K[("moeacc", e, NTT - 1, 1)]
            K[("moednlast", e)] = N_("cT")
            if DBG & 8:
                tok_dD[e] = sc.dma("y", lambda y, e_=e: y.dma_start(
                    out=dbgD[e_ * 128:(e_ + 1) * 128, :], in_=acc[:, 0:NTT * 1024]),
                    wait=[("cV", K[("moeacc", e, NTT - 1, 0)]),
                          ("cV", K[("moeacc", e, NTT - 1, 1)])])
            elif DBG & 16:
                tok_dD[e] = sc.dma("y", lambda y, e_=e: y.dma_start(
                    out=dbgD[e_ * 128:(e_ + 1) * 128, 0:8], in_=acc[:, 0:8]),
                    wait=[("cV", K[("moeacc", e, NTT - 1, 0)]),
                          ("cV", K[("moeacc", e, NTT - 1, 1)])])
            if e + 2 < E:
                w = [("cT", K[("moednlast", e)])]
                if (DBG & 1) and e + 2 == 2:
                    tok_dA = sc.dma("y", lambda y: y.dma_start(
                        out=dbgA[:, :], in_=ar16[:, 0:12288]), wait=tk_moew[0])
                    w.append(tok_dA)
                tk_moew.append(emit_wload(e + 2, w))
        cv_moe = N_("cV")
        ct_moe = N_("cT")

        if PH <= 4:
            return _early()

        XNP = 0
        for sb in range(SEQ // 512):
            blk = (sb * 512) // TPC
            scol = (sb * 512) % TPC
            xb = XNP + (sb % 2) * 4096
            w = [("dC", dc_ag), ("cT", ct_moe)]
            if sb >= 2:
                w.append(("cT", K.get(("p5t", sb - 2), 0)))
            tk5 = sc.dma("y", lambda y, b_=blk, s_=scol, x_=xb: y.dma_start(
                out=ar16[:, x_:x_ + 4096].rearrange("p (dc t) -> p dc t", dc=8),
                in_=ag_out[b_ * 1024:(b_ + 1) * 1024, s_:s_ + 512]
                    .rearrange("(dc p) t -> p dc t", p=128)), wait=w)
            for dc in range(8):
                wt = []
                if dc == 0:
                    wt = [tk5] + tk_attnw + [("cV", K.get(("p5ev", sb - 1), cv_moe))]
                E_("t", lambda t, d=dc, x_=xb: t.matmul(
                    pB[0:8, 0:512], wbf_s[:, d * 8:(d + 1) * 8],
                    ar16[:, x_ + d * 512: x_ + (d + 1) * 512],
                    start=(d == 0), stop=(d == 7)),
                   wait=wt, inc="cT" if dc == 7 else None)
            K[("p5t", sb)] = N_("cT")
            ct_z = K[("p5t", sb)]
            E_("v", lambda v: v.tensor_copy(zb8[0:8, 0:512], pB[0:8, 0:512]),
               wait=[("cT", ct_z)], inc="cV")
            K[("p5ev", sb)] = N_("cV")
            E_("s", lambda s: s.activation(zb8b[0:8, 0:512], zb8[0:8, 0:512], AF.Sigmoid),
               wait=[("cV", K[("p5ev", sb)])], inc="cS")
            E_("s", lambda s: s.activation(zb8b[0:8, 0:512], zb8b[0:8, 0:512], AF.Ln), inc="cS")
            cs_ln2 = N_("cS")
            E_("v", lambda v: v.tensor_scalar(
                zb8[0:8, 0:512], zb8b[0:8, 0:512], LOG2, None, ALU.add),
               wait=[("cS", cs_ln2)], inc="cV")
            for sg in range(4):
                E_("v", lambda v, s_=sg: v.tensor_tensor_scan(
                    zb8[0:4, s_ * 128:(s_ + 1) * 128], zb8b[0:4, s_ * 128:(s_ + 1) * 128],
                    zer8[0:4, 0:128], 0.0, ALU.add, ALU.add), inc="cV")
            cv_sc2 = N_("cV")
            for sg in range(4):
                jj = sb * 4 + sg
                pt_ = pC if sg % 2 == 0 else pD
                E_("t", lambda t, s_=sg, p_=pt_: t.transpose(
                    p_[:, 0:8], zb8[0:8, s_ * 128:(s_ + 1) * 128],
                    c_idf[0:8, 0:8]),
                   wait=([("cV", cv_sc2)] if sg == 0 else
                         [("cV", K.get(("lgev", sb, sg - 2), 0))]),
                   inc="cT")
                ctt = N_("cT")
                E_("v", lambda v, j_=jj, p_=pt_: v.tensor_copy(
                    lblgT[:, j_ * 8:(j_ + 1) * 8], p_[:, 0:8]),
                   wait=[("cT", ctt)], inc="cV")
                K[("lgev", sb, sg)] = N_("cV")
        cv_lblg = N_("cV")
        tk_rows = toks([
            sc.dma("g", lambda g: g.dma_start(
                out=mrow[0:1, :],
                in_=lblgT[63:64, :].rearrange("p (j x) -> p j x", x=8)[:, :, 0:4]),
                wait=[("cV", cv_lblg)]),
            sc.dma("g", lambda g: g.dma_start(
                out=lgcrow[0:1, :],
                in_=lblgT[127:128, :].rearrange("p (j x) -> p j x", x=8)[:, :, 0:4]))])
        E_("v", lambda v: v.memset(mnrow[0:1, :], 0.0), wait=tk_rows, inc="cV")
        E_("v", lambda v: v.tensor_copy(mnrow[0:1, 0:4 * NCH - 4], mrow[0:1, 4:4 * NCH]), inc="cV")
        E_("v", lambda v: v.tensor_tensor(
            d1a[:, :], lblgT[:, :].rearrange("p (j x) -> p j x", x=8)[:, :, 4:8],
            lblgT[:, :].rearrange("p (j x) -> p j x", x=8)[:, :, 0:4], ALU.subtract), inc="cV")
        VJ()
        cv_rows = N_("cV")
        E_("t", lambda t: t.matmul(pC[:, 0:4 * NCH], c_or[0:1, 0:128], mrow[0:1, :],
                                   start=True, stop=True), wait=[("cV", cv_rows)], inc="cT")
        ct_mb = N_("cT")
        E_("v", lambda v: v.tensor_tensor(swk[:, :], pC[:, 0:4 * NCH], d1a[:, :], ALU.add),
           wait=[("cT", ct_mb)], inc="cV")
        cv1 = N_("cV")
        E_("s", lambda s: s.activation(sbar[:, :], swk[:, :], AF.Exp),
           wait=[("cV", cv1)], inc="cS")
        cs1 = N_("cS")
        E_("v", lambda v: v.tensor_tensor(
            swk[:, :], lblgT[:, :].rearrange("p (j x) -> p j x", x=8)[:, :, 0:4],
            pC[:, 0:4 * NCH], ALU.subtract), wait=[("cS", cs1)], inc="cV")
        cv2 = N_("cV")
        E_("s", lambda s: s.activation(gga[:, :], swk[:, :], AF.Exp),
           wait=[("cV", cv2)], inc="cS")
        cs_gg = N_("cS")
        E_("v", lambda v: v.tensor_tensor(zzrow[0:1, :], mnrow[0:1, :], lgcrow[0:1, :], ALU.add),
           wait=[("cS", cs_gg)], inc="cV")
        cv_zz = N_("cV")
        E_("t", lambda t: t.matmul(pC[:, 256:256 + 4 * NCH], c_or[0:1, 0:128], zzrow[0:1, :],
                                   start=True, stop=True), wait=[("cV", cv_zz)], inc="cT")
        ct_zb2 = N_("cT")
        E_("v", lambda v: v.tensor_tensor(swk[:, :], pC[:, 256:256 + 4 * NCH], d1a[:, :], ALU.add),
           wait=[("cT", ct_zb2)], inc="cV")
        cv3 = N_("cV")
        E_("s", lambda s: s.activation(shk[:, :], swk[:, :], AF.Exp),
           wait=[("cV", cv3)], inc="cS")
        cs_shk = N_("cS")
        E_("v", lambda v: v.tensor_tensor(zzrow[0:1, :], zzrow[0:1, :], mrow[0:1, :], ALU.subtract),
           wait=[("cS", cs_shk)], inc="cV")
        cv_zz2 = N_("cV")
        E_("t", lambda t: t.matmul(pD[:, 0:4 * NCH], c_or[0:1, 0:128], zzrow[0:1, :],
                                   start=True, stop=True), wait=[("cV", cv_zz2)], inc="cT")
        ct_dk = N_("cT")
        E_("s", lambda s: s.activation(dkb[:, :], pD[:, 0:4 * NCH], AF.Exp),
           wait=[("cT", ct_dk)], inc="cS")
        cs_p5 = N_("cS")
        E_("v", lambda v: v.memset(ssf[:, :], 0.0), wait=[("cS", cs_p5)], inc="cV")
        E_("v", lambda v: v.memset(ssb[:, :], 0.0), inc="cV")
        cv_p5 = N_("cV")

        if PH <= 5:
            return _early()

        NGRP = NCH // 4
        for grp in range(NGRP):
            for jj in range(4):
                j = grp * 4 + jj
                blk = (j * C) // TPC
                scol = (j * C) % TPC
                xb = (j % 2) * 1024
                slot = j % 4
                w = [("dC", dc_ag), ("cS", cs_p5)]
                if j >= 2:
                    w.append(("cT", K.get(("dnproj", j - 2), 0)))
                tkxg = sc.dma("y", lambda y, b_=blk, s_=scol, x_=xb: y.dma_start(
                    out=xng[:, x_:x_ + 1024].rearrange("p (dc t) -> p dc t", dc=8),
                    in_=ag_out[b_ * 1024:(b_ + 1) * 1024, s_:s_ + 128]
                        .rearrange("(dc p) t -> p dc t", p=128)), wait=w)
                for dc in range(8):
                    wt = []
                    if dc == 0:
                        wt = [tkxg, ("cV", K.get(("kvev", j - 1), cv_p5)),
                              ("cS", K.get(("kex", j - 1), cs_p5))]
                    E_("t", lambda t, d=dc, x_=xb: t.matmul(
                        pA[:, 0:512], xng[:, x_ + d * 128: x_ + (d + 1) * 128],
                        wkv_s[:, d * 512:(d + 1) * 512],
                        start=(d == 0), stop=(d == 7)),
                       wait=wt, inc="cT" if dc == 7 else None)
                ct_kv = N_("cT")
                for ms in range(2):
                    for dc in range(8):
                        wt = []
                        if ms == 0 and dc == 0:
                            wt = [("cV", K.get(("qfev", j - 1), cv_p5)),
                                  ("cS", K.get(("qex", j - 1), cs_p5))]
                        E_("t", lambda t, d=dc, m_=ms, x_=xb: t.matmul(
                            pB[:, m_ * 128:(m_ + 1) * 128],
                            wq_s[:, d * 256 + m_ * 128: d * 256 + (m_ + 1) * 128],
                            xng[:, x_ + d * 128: x_ + (d + 1) * 128],
                            start=(d == 0), stop=(d == 7)),
                           wait=wt, inc="cT" if dc == 7 else None)
                ct_qf = N_("cT")
                K[("dnproj", j)] = ct_qf
                E_("s", lambda s: s.activation(esk[:, 0:256], pA[:, 0:256], AF.Exp,
                                               bias=0.0, scale=-1.0),
                   wait=[("cT", ct_kv)], inc="cS")
                K[("kex", j)] = N_("cS")
                E_("s", lambda s: s.activation(esk[:, 256:512], pB[:, 0:256], AF.Exp,
                                               bias=0.0, scale=-1.0),
                   wait=[("cT", ct_qf)], inc="cS")
                K[("qex", j)] = N_("cS")
                E_("v", lambda v: v.tensor_scalar(esk[:, 0:256], esk[:, 0:256], 1.0, None, ALU.add),
                   wait=[("cS", K[("kex", j)])], inc="cV")
                E_("v", lambda v: v.reciprocal(esk[:, 0:256], esk[:, 0:256]), inc="cV")
                E_("v", lambda v: v.tensor_tensor(kmf[:, :], pA[:, 0:256], esk[:, 0:256], ALU.mult),
                   inc="cV")
                cv_kmf = N_("cV")
                wq4 = [("cS", K[("qex", j)])]
                if j >= 4:
                    wq4.append(("cT", K.get(("omm", j - 4, 3), 0)))
                E_("v", lambda v: v.tensor_scalar(esk[:, 256:512], esk[:, 256:512], 1.0, None, ALU.add),
                   wait=wq4, inc="cV")
                E_("v", lambda v: v.reciprocal(esk[:, 256:512], esk[:, 256:512]), inc="cV")
                E_("v", lambda v, sl=slot: v.tensor_tensor(
                    qft[:, sl * 256:(sl + 1) * 256], pB[:, 0:256], esk[:, 256:512], ALU.mult),
                   inc="cV")
                K[("qfev", j)] = N_("cV")
                wv4 = []
                if j >= 4:
                    wv4.append(("cT", K.get(("zlast", j - 4), 0)))
                E_("v", lambda v, sl=slot: v.tensor_copy(
                    kvt[:, sl * 512 + 256: sl * 512 + 512], pA[:, 256:512]),
                   wait=wv4, inc="cV")
                K[("kvev", j)] = N_("cV")
                E_("v", lambda v: v.tensor_tensor(ksq[:, :], kmf[:, :], kmf[:, :], ALU.mult),
                   wait=[("cV", cv_kmf)], inc="cV")
                for hh in range(4):
                    E_("v", lambda v, h_=hh: v.reduce_sum(
                        knc[:, h_:h_ + 1], ksq[:, h_ * 64:(h_ + 1) * 64], AXX), inc="cV")
                cv_ks = N_("cV")
                E_("s", lambda s: s.activation(knc[:, 4:8], knc[:, 0:4], AF.Ln,
                                               bias=c_eps[:, 1:2], scale=1.0),
                   wait=[("cV", cv_ks)], inc="cS")
                E_("s", lambda s: s.activation(knc[:, 8:12], knc[:, 4:8], AF.Exp,
                                               bias=0.0, scale=-0.5), inc="cS")
                cs_kn = N_("cS")
                E_("v", lambda v, j_=j: v.tensor_tensor(
                    knc[:, 12:16], knc[:, 8:12], gga[:, j_ * 4:(j_ + 1) * 4], ALU.mult),
                   wait=[("cS", cs_kn)], inc="cV")
                E_("v", lambda v, j_=j: v.tensor_tensor(
                    knc[:, 4:8], knc[:, 8:12], shk[:, j_ * 4:(j_ + 1) * 4], ALU.mult), inc="cV")
                wk4 = []
                if j >= 4:
                    wk4.append(("cT", K.get(("supmm", j - 4, 3), 0)))
                    wk4.append(("cT", K.get(("kftp", j - 4), 0)))
                for hh in range(4):
                    E_("v", lambda v, h_=hh, sl=slot: v.tensor_scalar(
                        kvt[:, sl * 512 + h_ * 64: sl * 512 + (h_ + 1) * 64],
                        kmf[:, h_ * 64:(h_ + 1) * 64], knc[:, 12 + h_:13 + h_], None, ALU.mult),
                       wait=(wk4 if hh == 0 else []), inc="cV")
                    E_("v", lambda v, h_=hh, sl=slot: v.tensor_scalar(
                        kck[:, sl * 256 + h_ * 64: sl * 256 + (h_ + 1) * 64],
                        kmf[:, h_ * 64:(h_ + 1) * 64], knc[:, 4 + h_:5 + h_], None, ALU.mult),
                       inc="cV")
                    E_("v", lambda v, h_=hh, sl=slot: v.tensor_scalar(
                        knm[:, sl * 256 + h_ * 64: sl * 256 + (h_ + 1) * 64],
                        kmf[:, h_ * 64:(h_ + 1) * 64], knc[:, 8 + h_:9 + h_], None, ALU.mult),
                       inc="cV")
                cv_scl = N_("cV")
                for kk in range(2):
                    wt = []
                    if kk == 0:
                        wt = [("cV", cv_scl), ("cV", K.get(("kfev", j - 1), 0))]
                    E_("t", lambda t, k_=kk, sl=slot: t.transpose(
                        pTb[:, k_ * 128:(k_ + 1) * 128],
                        knm[:, sl * 256 + k_ * 128: sl * 256 + (k_ + 1) * 128], c_idb[:, :]),
                       wait=wt, inc="cT")
                K[("kftp", j)] = N_("cT")
                wkf = [("cT", K[("kftp", j)])]
                if j >= 4:
                    wkf.append(("cT", K.get(("omm", j - 4, 3), 0)))
                E_("v", lambda v, sl=slot: v.tensor_copy(kft[:, sl * 256:(sl + 1) * 256], pTb[:, 0:256]),
                   wait=wkf, inc="cV")
                K[("kfev", j)] = N_("cV")

            for jj in range(4):
                j = grp * 4 + jj
                slot = j % 4
                for h in range(HPC):
                    par = (h % 2) * 64
                    qsl = (h // 2) * 128
                    prev_ea = K.get(("eaex", j, h - 1), K.get(("eaex", j - 1, 3), 0))
                    E_("v", lambda v, j_=j, h_=h: v.tensor_scalar(
                        cbt[:, :], c_m1[:, :], d1a[:, j_ * 4 + h_:j_ * 4 + h_ + 1], None, ALU.add),
                       wait=[("cV", K[("kfev", j)]), ("cS", prev_ea)], inc="cV")
                    cv_cb1 = N_("cV")
                    E_("t", lambda t: t.transpose(pC[:, 0:128], cbt[:, :], c_idf[:, :]),
                       wait=[("cV", cv_cb1)], inc="cT")
                    ct_c1 = N_("cT")
                    E_("s", lambda s, j_=j, h_=h: s.activation(
                        ea_t[:, 0:128], pC[:, 0:128], AF.Exp,
                        bias=lblgT[:, j_ * 8 + h_:j_ * 8 + h_ + 1], scale=1.0),
                       wait=[("cT", ct_c1)], inc="cS")
                    cs_e1 = N_("cS")
                    E_("v", lambda v, j_=j, h_=h: v.tensor_scalar(
                        cbt[:, :], c_m2[:, :], lblgT[:, j_ * 8 + h_:j_ * 8 + h_ + 1],
                        None, ALU.add), wait=[("cS", cs_e1)], inc="cV")
                    cv_cb2 = N_("cV")
                    E_("t", lambda t: t.transpose(pC[:, 128:256], cbt[:, :], c_idf[:, :]),
                       wait=[("cV", cv_cb2)], inc="cT")
                    ct_c2 = N_("cT")
                    E_("s", lambda s, j_=j, h_=h: s.activation(
                        ea_t[:, 128:256], pC[:, 128:256], AF.Exp,
                        bias=d1a[:, j_ * 4 + h_:j_ * 4 + h_ + 1], scale=1.0),
                       wait=[("cT", ct_c2)], inc="cS")
                    K[("eaex", j, h)] = N_("cS")
                    cs_e2 = K[("eaex", j, h)]
                    xet = K.get(("xxevtok", j, h - 1))
                    wg = [xet] if xet else [("cV", K[("kfev", j)])]
                    if h == 0:
                        wg.append(("cV", K.get(("msev", j - 1), 0)))
                    E_("t", lambda t, sl=slot, p_=par, q_=qsl: t.matmul(
                        pD[:, 0:128],
                        kft[p_:p_ + 64, sl * 256 + q_: sl * 256 + q_ + 128],
                        kft[p_:p_ + 64, sl * 256 + q_: sl * 256 + q_ + 128],
                        start=True, stop=True),
                       wait=wg, inc="cT")
                    E_("t", lambda t, sl=slot, p_=par, q_=qsl: t.matmul(
                        pD[:, 128:256],
                        kft[p_:p_ + 64, sl * 256 + q_: sl * 256 + q_ + 128],
                        qft[p_:p_ + 64, sl * 256 + q_: sl * 256 + q_ + 128],
                        start=True, stop=True), inc="cT")
                    ct_g = N_("cT")
                    E_("v", lambda v: v.tensor_tensor(xx[:, :], pD[:, 0:128], ea_t[:, 0:128], ALU.mult),
                       wait=[("cT", ct_g), ("cS", cs_e2)], inc="cV")
                    E_("v", lambda v: v.tensor_tensor(xxt[:, :], pD[:, 0:128], ea_t[:, 128:256],
                                                      ALU.mult), inc="cV")
                    E_("v", lambda v, j_=j, h_=h: v.scalar_tensor_tensor(
                        mt[:, :], pD[:, 128:256], sbar[:, j_ * 4 + h_:j_ * 4 + h_ + 1],
                        c_incl[:, :], ALU.mult, ALU.mult), inc="cV")
                    cv_xe = N_("cV")
                    sq_ev = []
                    for (si, (la_, ra_, dst)) in enumerate([
                            (xxt, xx, a2), (xx, xxt, a2t), (a2t, a2, a4),
                            (a2, a2t, a4t), (a4, a4t, a8t)]):
                        sqp = pE if si % 2 == 0 else pD
                        E_("t", lambda t, l_=la_, r_=ra_, p_=sqp: t.matmul(
                            p_[:, 0:128], l_[:, :], r_[:, :],
                            start=True, stop=True),
                           wait=([("cV", cv_xe)] if si < 2 else [sq_ev[si - 2]]), inc="cT")
                        cts = N_("cT")
                        E_("s", lambda s, d_=dst, p_=sqp: s.activation(
                            d_[:, :], p_[:, 0:128], AF.Copy),
                           wait=[("cT", cts)], inc="cS")
                        sq_ev.append(("cS", N_("cS")))
                    K[("xxevtok", j, h)] = sq_ev[3]
                    rap = (lambda sl, h_: (lambda: kvt[:, sl * 512:(sl + 1) * 512]
                                           .rearrange("p (x b c) -> p x b c", x=2, b=4)[:, :, h_, :]))(slot, h)
                    zprev = 0
                    for (si, (la_, rin, dst)) in enumerate([
                            (a8t, None, zz1), (a4t, zz1, zz2), (a2t, zz2, zz1), (xxt, zz1, zz2)]):
                        E_("t", lambda t, l_=la_, r_=rin, s_=si, ra=rap: t.matmul(
                            pE[:, (s_ % 4) * 128:(s_ % 4) * 128 + 128], l_[:, :],
                            (ra() if r_ is None else r_[:, :]),
                            start=True, stop=True),
                           wait=([sq_ev[4]] if si == 0 else [("cV", zprev)]), inc="cT")
                        ctz = N_("cT")
                        if si == 0:
                            K[("zfirst", j, h)] = ctz
                        op = ALU.add if si < 3 else ALU.subtract
                        if si < 3:
                            E_("v", lambda v, d_=dst, r_=rin, s_=si, ra=rap, o_=op: v.tensor_tensor(
                                d_[:, :], (ra() if r_ is None else r_[:, :]),
                                pE[:, (s_ % 4) * 128:(s_ % 4) * 128 + 128], o_),
                               wait=[("cT", ctz)], inc="cV")
                        else:
                            E_("v", lambda v, d_=dst, r_=rin, s_=si: v.tensor_tensor(
                                d_[:, :], r_[:, :],
                                pE[:, (s_ % 4) * 128:(s_ % 4) * 128 + 128], ALU.subtract),
                               wait=[("cT", ctz)], inc="cV")
                        zprev = N_("cV")
                        if si == 3:
                            K[("zlastmm", j, h)] = ctz
                    cv_z = zprev
                    wwf = [("cV", cv_z)]
                    if h == 0:
                        wwf.append(("cV", K.get(("ontev", j - 1, 3), 0)))
                    E_("t", lambda t, p_=par: t.transpose(
                        pTb2[p_:p_ + 64, 0:128], zz2[:, 0:64], c_idb[:, :],
                        tile_position=(0, p_)),
                       wait=wwf, inc="cT")
                    ct_wf = N_("cT")
                    E_("s", lambda s, p_=par: s.activation(
                        wfb[p_:p_ + 64, 0:128], pTb2[p_:p_ + 64, 0:128], AF.Copy),
                       wait=[("cT", ct_wf)], inc="cS")
                    cs_wf = N_("cS")
                    E_("t", lambda t, p_=par, h_=h: t.matmul(
                        pE[:, 128:192], wfb[p_:p_ + 64, 0:128],
                        ssb[p_:p_ + 64, (h_ // 2) * 64:(h_ // 2) * 64 + 64],
                        start=True, stop=True),
                       wait=[("cS", cs_wf)], inc="cT")
                    ct_u = N_("cT")
                    E_("v", lambda v: v.tensor_tensor(uu[:, :], zz2[:, 64:128], pE[:, 128:192],
                                                      ALU.subtract),
                       wait=[("cT", ct_u)], inc="cV")
                    cv_u = N_("cV")
                    E_("t", lambda t, h_=h: t.matmul(
                        pF[0:64, h_ * 128:(h_ + 1) * 128], uu[:, :], mt[:, :],
                        start=True, stop=False),
                       wait=[("cV", cv_u)], inc=None)
                    E_("t", lambda t, h_=h, p_=par, q_=qsl, sl=slot: t.matmul(
                        pF[0:64, h_ * 128:(h_ + 1) * 128],
                        ssb[p_:p_ + 64, (h_ // 2) * 64:(h_ // 2) * 64 + 64],
                        qft[p_:p_ + 64, sl * 256 + q_: sl * 256 + q_ + 128],
                        start=False, stop=True), inc="cT")
                    K[("omm", j, h)] = N_("cT")
                    E_("t", lambda t, h_=h, p_=par, sl=slot: t.matmul(
                        pE[p_:p_ + 64, 256 + (h_ // 2) * 64:256 + (h_ // 2) * 64 + 64],
                        kck[:, sl * 256 + h_ * 64: sl * 256 + (h_ + 1) * 64], uu[:, :],
                        start=True, stop=True,
                        tile_position=(0, p_)),
                       inc="cT")
                    K[("supmm", j, h)] = N_("cT")
                    ct_s = K[("supmm", j, h)]
                    E_("v", lambda v, h_=h, p_=par, j_=j: v.scalar_tensor_tensor(
                        ssf[p_:p_ + 64, (h_ // 2) * 64:(h_ // 2) * 64 + 64],
                        ssf[p_:p_ + 64, (h_ // 2) * 64:(h_ // 2) * 64 + 64],
                        dkb[p_:p_ + 64, j_ * 4 + h_:j_ * 4 + h_ + 1],
                        pE[p_:p_ + 64, 256 + (h_ // 2) * 64:256 + (h_ // 2) * 64 + 64],
                        ALU.mult, ALU.add),
                       wait=[("cT", ct_s)], inc="cV")
                    VJ()
                    E_("v", lambda v, h_=h, p_=par: v.tensor_copy(
                        ssb[p_:p_ + 64, (h_ // 2) * 64:(h_ // 2) * 64 + 64],
                        ssf[p_:p_ + 64, (h_ // 2) * 64:(h_ // 2) * 64 + 64]), inc="cV")
                    K[("sup", j, h)] = N_("cV")
                K[("zlast", j)] = K[("zlastmm", j, 3)]
                ct_o_all = K[("omm", j, 3)]
                E_("s", lambda s: s.activation(offb[:, :], pF[0:64, 0:512], AF.Copy),
                   wait=[("cT", ct_o_all)], inc="cS")
                cv_off = ("cS", N_("cS"))
                oslot = (j % 2) * 256
                for h in range(HPC):
                    wt = []
                    if h == 0:
                        wt = [cv_off, ("cV", K.get(("ontev", j - 1, 3), 0))]
                    E_("t", lambda t, h_=h: t.transpose(
                        pTb2[:, 256 + h_ * 64:256 + (h_ + 1) * 64],
                        offb[:, h_ * 128:(h_ + 1) * 128], c_idb[0:64, 0:64]),
                       wait=wt, inc="cT")
                ct_ot = N_("cT")
                for h in range(HPC):
                    wt = [("cT", ct_ot)] if h == 0 else []
                    if h == 0 and j >= 2:
                        wt.append(K[("odma", j - 2)])
                    E_("v", lambda v, h_=h, os_=oslot, j_=j: v.tensor_scalar(
                        ont[:, os_ + h_ * 64: os_ + (h_ + 1) * 64],
                        pTb2[:, 256 + h_ * 64:256 + (h_ + 1) * 64],
                        gga[:, j_ * 4 + h_:j_ * 4 + h_ + 1], None, ALU.mult),
                       wait=wt, inc="cV")
                    K[("ontev", j, h)] = N_("cV")
                E_("s", lambda s, os_=oslot: s.activation(osq[0:128, 0:256] if False else
                                                          ar32[:, 2560:2816],
                                                          ont[:, os_:os_ + 256], AF.Square),
                   wait=[("cV", K[("ontev", j, 3)])], inc="cS")
                cs_sq2 = N_("cS")
                for h in range(HPC):
                    E_("v", lambda v, h_=h: v.reduce_sum(
                        msc[:, h_:h_ + 1], ar32[:, 2560 + h_ * 64:2560 + (h_ + 1) * 64], AXX),
                       wait=([("cS", cs_sq2)] if h == 0 else []), inc="cV")
                K[("msev", j)] = N_("cV")
                E_("s", lambda s: s.activation(msc[:, 8:12], msc[:, 0:4], AF.Ln,
                                               bias=c_eps[:, 2:3], scale=1.0 / HD),
                   wait=[("cV", K[("msev", j)])], inc="cS")
                E_("s", lambda s: s.activation(msc[:, 4:8], msc[:, 8:12], AF.Exp,
                                               bias=0.0, scale=-0.5), inc="cS")
                cs_c = N_("cS")
                for h in range(HPC):
                    E_("v", lambda v, h_=h, os_=oslot: v.tensor_scalar(
                        ont[:, os_ + h_ * 64: os_ + (h_ + 1) * 64],
                        ont[:, os_ + h_ * 64: os_ + (h_ + 1) * 64],
                        msc[:, 4 + h_:5 + h_], None, ALU.mult),
                       wait=([("cS", cs_c)] if h == 0 else []), inc="cV")
                cv_ont = N_("cV")
                K[("odma", j)] = sc.dma("y", lambda y, j_=j, os_=oslot: y.dma_start(
                    out=o_dram[j_ * 128:(j_ + 1) * 128, :], in_=ont[:, os_:os_ + 256]),
                    wait=[("cV", cv_ont)])
        cv_dn = N_("cV")
        tk_dn = toks([K[("odma", j)] for j in range(NCH)])
        ct_dn = N_("cT")

        if PH <= 6:
            return _early()

        for i in range(SEQ // 128):
            sl = (i % 2) * 256
            w = list(tk_dn)
            if i >= 2:
                w.append(("cT", K.get(("wot", i - 2), 0)))
            tko = sc.dma("y", lambda y, i_=i, s_=sl: y.dma_start(
                out=onr[:, s_:s_ + 256], in_=o_dram[i_ * 128:(i_ + 1) * 128, :]), wait=w)
            for kk in range(2):
                wt = []
                if kk == 0:
                    wt = [tko, ("cV", K.get(("onfev", i - 1), cv_dn))]
                E_("t", lambda t, k_=kk, s_=sl: t.transpose(
                    pTb[:, k_ * 128:(k_ + 1) * 128], onr[:, s_ + k_ * 128: s_ + (k_ + 1) * 128],
                    c_idb[:, :]),
                   wait=wt, inc="cT")
            ct_onf = N_("cT")
            E_("v", lambda v, s_=sl: v.tensor_copy(onf[:, s_:s_ + 256], pTb[:, 0:256]),
               wait=[("cT", ct_onf)], inc="cV")
            cv_onf = N_("cV")
            K[("onfev", i)] = cv_onf
            for half in range(2):
                pd_ = pC if half == 0 else pD
                for kk in range(2):
                    wt = []
                    if kk == 0:
                        wt = [("cV", cv_onf), ("cV", K.get(("wopev", i - 1, half), 0))]
                    E_("t", lambda t, k_=kk, h_=half, s_=sl, po=pd_: t.matmul(
                        po[:, 0:512], onf[:, s_ + k_ * 128: s_ + (k_ + 1) * 128],
                        wo_s[:, k_ * 1024 + h_ * 512: k_ * 1024 + h_ * 512 + 512],
                        start=(k_ == 0), stop=(k_ == 1)),
                       wait=wt, inc="cT" if kk == 1 else None)
                ct_wo = N_("cT")
                K[("wot", i)] = ct_wo
                wwop = [("cT", ct_wo)]
                if i >= 2:
                    wwop.append(K[("rsin", i - 2)])
                E_("v", lambda v, h_=half, i_=i, po=pd_: v.tensor_copy(
                    wop[:, (i_ % 2) * 1024 + h_ * 512: (i_ % 2) * 1024 + (h_ + 1) * 512],
                    po[:, 0:512]),
                   wait=wwop, inc="cV")
                K[("wopev", i, half)] = N_("cV")
            cv_wop = N_("cV")
            K[("rsin", i)] = sc.dma("y", lambda y, i_=i: y.dma_start(
                out=rs_in[i_ * 128:(i_ + 1) * 128, :],
                in_=wop[:, (i_ % 2) * 1024:(i_ % 2) * 1024 + 1024]),
                wait=[("cV", cv_wop)])
        tk_rsin = toks([K[("rsin", i)] for i in range(SEQ // 128)])
        E_("g", lambda g: g.collective_compute(
            "ReduceScatter", ALU.add, replica_groups=GROUPS,
            ins=[rs_in[:, :]], outs=[rs_out[:, :]]),
           wait=tk_rsin, inc="dC")
        dc_rs = N_("dC")

        if PH <= 7:
            return _early()

        tk_out = []
        for i in range(NTT):
            sl = (i % 2) * 1024
            w = [("dC", dc_rs)]
            if i >= 2:
                w.append(("cV", K.get(("fin", i - 2), 0)))
            tkr = sc.dma("y", lambda y, i_=i, s_=sl: y.dma_start(
                out=rsb[:, s_:s_ + 1024], in_=rs_out[i_ * 128:(i_ + 1) * 128, :]), wait=w)
            E_("v", lambda v, i_=i, s_=sl: v.tensor_tensor(
                acc[:, i_ * 1024:(i_ + 1) * 1024], acc[:, i_ * 1024:(i_ + 1) * 1024],
                rsb[:, s_:s_ + 1024], ALU.add),
               wait=[tkr, ("cV", cv_moe)], inc="cV")
            cvf = N_("cV")
            K[("fin", i)] = cvf
            tk_out.append(sc.dma("g", lambda g, i_=i: g.dma_start(
                out=out[i_ * 128:(i_ + 1) * 128, :], in_=acc[:, i_ * 1024:(i_ + 1) * 1024]),
                wait=[("cV", cvf)]))
        E_("g", lambda g: None, wait=toks(tk_out), inc=None)

        _attach()
    return nc


def kernel(**inputs):
    x = np.ascontiguousarray(np.asarray(inputs["L0_attn_x"], np.float32).reshape(B * SEQ, D))
    Wq = np.asarray(inputs["L0_attn_Wq"], np.float32)
    Wk = np.asarray(inputs["L0_attn_Wk"], np.float32)
    Wv = np.asarray(inputs["L0_attn_Wv"], np.float32)
    Wo = np.asarray(inputs["L0_attn_Wo"], np.float32)
    Wbet = np.asarray(inputs["L0_attn_Wbet"], np.float32)
    Wf = np.asarray(inputs["L0_attn_Wf"], np.float32)
    norm_w = np.asarray(inputs["L0_attn_norm_w"], np.float32)
    gwT = np.ascontiguousarray(np.asarray(inputs["L0_ffn_gate_weight"], np.float32).T)
    gate_b = np.asarray(inputs["L0_ffn_e_score_correction_bias"], np.float32)
    gpw = np.ascontiguousarray(np.asarray(inputs["L0_ffn_gate_proj"], np.float32).transpose(0, 2, 1))
    upw = np.ascontiguousarray(np.asarray(inputs["L0_ffn_up_proj"], np.float32).transpose(0, 2, 1))
    dpw = np.ascontiguousarray(np.asarray(inputs["L0_ffn_down_proj"], np.float32).transpose(0, 2, 1))

    nc = build_program()
    in_maps = []
    for c in range(NCORE):
        hg = c % 4
        cols = slice(hg * HCOL, (hg + 1) * HCOL)
        hcols = slice(hg * HPC, (hg + 1) * HPC)
        in_maps.append({
            "x_in": np.ascontiguousarray(x[c * TPC:(c + 1) * TPC]),
            "wq": np.ascontiguousarray(Wq[:, cols]),
            "wk": np.ascontiguousarray(Wk[:, cols]),
            "wv": np.ascontiguousarray(Wv[:, cols]),
            "wbf": np.ascontiguousarray(np.concatenate([Wf[:, hcols], Wbet[:, hcols]], axis=1)),
            "wo_sl": np.ascontiguousarray(Wo[cols, :]),
            "nw_sl": np.ascontiguousarray(norm_w[cols][None, :]),
            "gw": gwT,
            "gbias": np.ascontiguousarray(gate_b[None, :]),
            "gp": gpw, "up": upw, "dp": dpw,
        })
    res = run_bass_kernel_spmd(nc, in_maps, core_ids=list(range(NCORE)))
    parts = [np.asarray(res.results[c]["out"], np.float32) for c in range(NCORE)]
    return np.concatenate(parts, axis=0).reshape(B, SEQ, D)
